# version 23
# baseline (speedup 1.0000x reference)
import numpy as np

import concourse.bass as bass
import concourse.mybir as mybir
from concourse.bass import IndirectOffsetOnAxis
from concourse.bass_utils import run_bass_kernel_spmd

F32 = mybir.dt.float32
BF16 = mybir.dt.bfloat16
AF = mybir.ActivationFunctionType
OP = mybir.AluOpType
AX = mybir.AxisListType

D = 1024
HEADS = 16
KVH = 4
HD = 64
HALF = 32
NE = 8
HID = 2816
NHC = HID // 128
EPS_LN = 1e-5
EPS_RMS = 1e-6
SCALE = HALF ** -0.5


def cap_for(T):
    return max(128, ((5 * T // 16 + 127) // 128) * 128)


class Ctr:

    def __init__(self, sem, step=1):
        self.sem = sem
        self.step = step
        self.n = 0

    def tick(self, instr, k=None):
        k = self.step if k is None else k
        instr.then_inc(self.sem, k)
        self.n += k
        return self.n

    def after(self, eng, at=None):
        v = self.n if at is None else at
        if v > 0:
            eng.wait_ge(self.sem, v)


def chunks(total, size):
    out, off = [], 0
    while off < total:
        out.append((off, min(size, total - off)))
        off += size
    return out


def build_program(T=2048, upto="full", dbg=(), ne=NE):

    dbg = dict(dbg)
    NT = T // 128
    THALF = T // 2
    NTH = NT // 2
    NQB = T // 256

    nc = bass.Bass("TRN2", target_bir_lowering=False, num_swdge_queues=2)

    x_d = nc.dram_tensor("x", [T, D], F32, kind="ExternalInput").ap()
    wqt_d = nc.dram_tensor("wqt", [D, D], F32, kind="ExternalInput").ap()
    wkvt_d = nc.dram_tensor("wkvt", [D, 512], F32, kind="ExternalInput").ap()
    woutt_d = nc.dram_tensor("woutt", [D, D], F32, kind="ExternalInput").ap()
    lam_d = nc.dram_tensor("lam", [HEADS], F32, kind="ExternalInput").ap()
    rmsw_d = nc.dram_tensor("rmsw", [HD], F32, kind="ExternalInput").ap()
    gw_d = nc.dram_tensor("gw", [NE, D], F32, kind="ExternalInput").ap()
    w1t_d = nc.dram_tensor("w1t", [NE, D, HID], F32, kind="ExternalInput").ap()
    w3t_d = nc.dram_tensor("w3t", [NE, D, HID], F32, kind="ExternalInput").ap()
    w2t_d = nc.dram_tensor("w2t", [NE, HID, D], F32, kind="ExternalInput").ap()
    ident_d = nc.dram_tensor("ident", [128, 128], F32, kind="ExternalInput").ap()
    cos_d = nc.dram_tensor("cosc", [T, 16], F32, kind="ExternalInput").ap()
    sin_d = nc.dram_tensor("sinc", [T, 16], F32, kind="ExternalInput").ap()
    mask_d = nc.dram_tensor("maskc", [128, 512], F32, kind="ExternalInput").ap()
    ones_d = nc.dram_tensor("onesc", [1, 128], F32, kind="ExternalInput").ap()
    ltri_d = nc.dram_tensor("ltri", [128, 128], F32, kind="ExternalInput").ap()
    onesq_d = nc.dram_tensor("onesq", [128, 128], F32, kind="ExternalInput").ap()
    tokid_d = nc.dram_tensor("tokid", [128, T // 128], mybir.dt.int32, kind="ExternalInput").ap()
    ecap_d = nc.dram_tensor("ecap", [128, 8], F32, kind="ExternalInput").ap()
    pfx_d = nc.dram_tensor("pfx", [128, 128], F32, kind="ExternalInput").ap()
    CAP = cap_for(T)
    NCT = CAP // 128
    NSLOT = NE * CAP
    _xnc_h = nc.dram_tensor("xnc", [T + 1, 1040], BF16, kind="Internal")
    xnc_d = _xnc_h.ap()
    xnc_f = _xnc_h.bitcast(F32).ap()
    idx_d = nc.dram_tensor("idxs", [NSLOT + 128], mybir.dt.int32, kind="Internal").ap()
    g_d = nc.dram_tensor("gaccd", [T + 1, D], F32, kind="Internal").ap()
    out_d = nc.dram_tensor("out", [T, D], F32, kind="ExternalOutput").ap()
    dbg_d = {}
    for name, shape in dbg.items():
        dbg_d[name] = nc.dram_tensor("dbg_" + name, list(shape), F32, kind="ExternalOutput").ap()

    _CAP = cap_for(T)
    moe_arm = (HID // 128) * _CAP + 8192 + (HID // 128) * 1024 \
        + (_CAP // 128) * 1040 + 2 * 8 * _CAP + 8192
    AR = max(20 * T + (T // 128) * 260 + 20480, moe_arm) + 4096
    if 12 * T < 8192:
        AR += 8192
    from contextlib import ExitStack
    with ExitStack() as _stk:
        def _sb(name, shape, dt):
            return _stk.enter_context(nc.sbuf_tensor(name, shape, dt))

        xnT = _sb("xnT", [128, 8 * T], BF16)
        arena = _sb("arena", [128, AR], BF16)
        expb = _sb("expb", [128, 3072], BF16)
        scr = _sb("scr", [128, 1024], F32)
        stat = _sb("stat", [128, 64], F32)
        gsc = _sb("gsc", [128, 32], F32)
        cbig = _sb("cbig", [128, NT * 8], F32)
        identb = _sb("identb", [128, 128], BF16)
        cosb = _sb("cosb", [128, NT * 16], F32)
        sinb = _sb("sinb", [128, NT * 16], F32)
        maskb = _sb("maskb", [128, 512], BF16)
        zrow = _sb("zrow", [1, 512], BF16)
        lamb = _sb("lamb", [128, 16], F32)
        lam0 = _sb("lam0", [128, 16], F32)
        ltri_s = _sb("ltris", [128, 128], F32)
        onesq_s = _sb("onesqs", [128, 128], F32)
        tokid_sb = _sb("tokid_sb", [128, NT], mybir.dt.int32)
        ecap_sb = _sb("ecap_sb", [128, 8], F32)
        pfx_s = _sb("pfx_s", [128, 128], F32)
        msk = _sb("msk", [128, NT * 8], F32)
        rall = _sb("rall", [128, NT * 8], F32)
        sidx = _sb("sidx", [128, NT * 2], mybir.dt.int32)
        ibig = _sb("ibig", [128, NSLOT // 128], mybir.dt.int32)
        idx_sb = _sb("idx_sb", [128, NSLOT // 128], mybir.dt.int32)
        idxstg = _sb("idxstg", [NSLOT // 128, 128], mybir.dt.int32)
        idxstf = _sb("idxstf", [NSLOT // 128, 128], F32)
        idxf2 = _sb("idxf2", [128, NSLOT // 128], F32)
        cg_sb = _sb("cg_sb", [128, 2 * NCT], F32)
        rwb = _sb("rwb", [128, 64], F32)
        identf = _sb("identf", [128, 128], F32)
        gateTf = _sb("gateTf", [128, 64], F32)
        epsc = _sb("epsc", [128, 2], F32)
        def _sem(name):
            return _stk.enter_context(nc.semaphore(name))

        CQ = Ctr(_sem("smCQ"), 16)
        CG = Ctr(_sem("smCG"), 16)
        X = [Ctr(_sem("smX0"), 16), Ctr(_sem("smX1"), 16)]
        W13 = [Ctr(_sem("smWa"), 16), Ctr(_sem("smWb"), 16),
               Ctr(_sem("smWc"), 16), Ctr(_sem("smWd"), 16)]
        STG = Ctr(_sem("smSTG"), 16)
        SQT = Ctr(_sem("smSQT"), 16)
        SOT = Ctr(_sem("smSOT"), 16)
        OUT = Ctr(_sem("smOUT"), 16)
        DBG = Ctr(_sem("smDBG"), 16)
        sem_t = _sem("smT"); sem_v = _sem("smV"); sem_s = _sem("smS")
        TE = Ctr(sem_t)
        V = Ctr(sem_v)
        S = Ctr(sem_s)

        class _Gated:

            def __init__(self, eng, ctr):
                self._e = eng
                self._c = ctr

            def __getattr__(self, name):
                f = getattr(self._e, name)
                if name in ("wait_ge",):
                    return f

                def g(*a, **k):
                    self._c.after(self._e)
                    return f(*a, **k)
                return g

        te, sy, gp = nc.tensor, nc.sync, nc.gpsimd
        ve = _Gated(nc.vector, V)
        se = _Gated(nc.scalar, S)

        CG.tick(gp.dma_start(identb[:], ident_d))
        CG.tick(gp.dma_start(maskb[:], mask_d))
        CQ.tick(sy.dma_start(cosb[:].rearrange("p (n f) -> p n f", f=16),
                             cos_d.rearrange("(n p) f -> p n f", p=128)))
        CQ.tick(sy.dma_start(sinb[:].rearrange("p (n f) -> p n f", f=16),
                             sin_d.rearrange("(n p) f -> p n f", p=128)))
        CQ.tick(sy.dma_start(identf[:], ident_d))
        CQ.tick(sy.dma_start(scr[0:8, 0:1024], gw_d[:, :]))
        CQ.tick(sy.dma_start(ltri_s[:], ltri_d))
        CQ.tick(sy.dma_start(onesq_s[:], onesq_d))
        CQ.tick(sy.dma_start(tokid_sb[:], tokid_d))
        CQ.tick(sy.dma_start(ecap_sb[:], ecap_d))
        CQ.tick(sy.dma_start(pfx_s[:], pfx_d))
        V.tick(ve.memset(zrow[:], 0.0))
        V.tick(ve.memset(epsc[:, 0:1], EPS_LN))
        V.tick(ve.memset(epsc[:, 1:2], EPS_RMS))

        CQ.tick(sy.dma_start(lam0[:], lam_d.rearrange("(a b) -> a b", a=1).broadcast_to([128, 16])))
        CQ.tick(sy.dma_start(rwb[:], rmsw_d.rearrange("(a b) -> a b", a=1).broadcast_to([128, 64])))
        CQ.after(se)
        S.tick(se.activation(lamb[:], lam0[:], AF.Sigmoid))

        with nc.psum_tensor("pg", [128, 64], F32) as pg:
            CQ.after(te)
            S.after(te)
            V.after(te)
            for kc in range(8):
                TE.tick(te.transpose(pg[0 : 128, kc * 8 : kc * 8 + 8], scr[0:8, kc * 128 : (kc + 1) * 128], identf[0:8, 0:8]))
            TE.after(ve)
            V.tick(ve.tensor_copy(gateTf[:], pg[:]))
        CQ.after(ve)
        V.tick(ve.memset(scr[:], 0.0))
        V.tick(ve.memset(ibig[:], T))
        gate_t_ready = V.n
        GZ = Ctr(_sem("smGZ"), 16)
        V.after(sy)
        for tt in range(NT):
            GZ.tick(sy.dma_start(g_d[tt * 128 : (tt + 1) * 128, :], scr[:]))
        GZ.tick(sy.dma_start(g_d[T : T + 1, :], scr[0:1, :]))
        GZ.tick(sy.dma_start(xnc_d[T : T + 1, :], scr.bitcast(BF16)[0:1, 0:1040]))

        CG.after(te)
        CG.after(ve)
        _xt_off = NHC * CAP + ((NHC * CAP) & 1)
        xt = arena[:, _xt_off : _xt_off + 4096].bitcast(F32)
        scr2 = arena[:, _xt_off + 4096 : _xt_off + 6144].bitcast(F32)
        outf = arena[:, _xt_off + 6144 : _xt_off + 8192].bitcast(F32)
        _sb2 = NHC * CAP + 8192 + ((NHC * CAP) & 1)
        scr2b = arena[:, _sb2 : _sb2 + 2048].bitcast(F32)
        scrv = [scr2, scr2b]
        XND = [Ctr(_sem("smXN0"), 16), Ctr(_sem("smXN1"), 16)]
        CD = Ctr(_sem("smCD"), 16)
        GZ.after(ve)
        with nc.psum_tensor("pt0", [128, 1024], BF16) as pt0, \
             nc.psum_tensor("pt1", [128, 1024], BF16) as pt1, \
             nc.psum_tensor("ptf", [128, 1024], F32) as ptf, \
             nc.psum_tensor("pga", [128, 8], F32) as pga:
            xload = []
            for tt in range(min(2, NT)):
                X[tt % 2].tick(sy.dma_start(xt[:, (tt % 2) * 1024 : (tt % 2 + 1) * 1024],
                                            x_d[tt * 128 : (tt + 1) * 128, :]))
                xload.append(X[tt % 2].n)
            evac_done = [0, 0]
            gate_read = 0
            xnf_ev = 0
            sub_done = {}
            st1_done = {}
            bf16t_done = {}
            f32t_done = {}

            def stage1(tt):
                b = tt % 2
                sb = 16 * b
                xs = xt[:, b * 1024 : (b + 1) * 1024]
                X[b].after(ve, xload[tt])
                V.tick(ve.reduce_sum(stat[:, sb : sb + 1], xs, axis=AX.X))
                V.tick(ve.scalar_tensor_tensor(
                    scr[:], stat[:, sb : sb + 1].broadcast_to([128, 1024]),
                    -1.0 / D, xs, OP.mult, OP.add))
                sub_done[tt] = V.n
                if tt - 2 in f32t_done:
                    TE.after(ve, f32t_done[tt - 2])
                V.tick(ve.scalar_tensor_tensor(scrv[b][:], scr[:], 1.0, scr[:],
                                               OP.mult, OP.mult,
                                               accum_out=stat[:, sb + 2 : sb + 3]))
                V.after(se)
                S.tick(se.activation(stat[:, sb + 3 : sb + 4], stat[:, sb + 2 : sb + 3],
                                     AF.Sqrt, bias=epsc[:, 0:1], scale=1.0 / D))
                S.after(ve)
                V.tick(ve.reciprocal(stat[:, sb + 4 : sb + 5], stat[:, sb + 3 : sb + 4]))
                xn_tok = expb[:, b * 1024 : b * 1024 + 1024]
                XND[b].after(ve)
                if tt - 2 in bf16t_done:
                    TE.after(ve, bf16t_done[tt - 2])
                V.tick(ve.tensor_scalar(xn_tok, scr[:], stat[:, sb + 4 : sb + 5], None, OP.mult))
                V.tick(ve.tensor_scalar(scrv[b][:], scr[:], stat[:, sb + 4 : sb + 5], None, OP.mult))
                st1_done[tt] = V.n

            def stage2(tt):
                nonlocal gate_read, xnf_ev
                b = tt % 2
                pt = pt0 if b == 0 else pt1
                xn_tok = expb[:, b * 1024 : b * 1024 + 1024]
                V.after(te, st1_done[tt])
                V.after(te, evac_done[b])
                for kc in range(8):
                    TE.tick(te.transpose(pt[:, kc * 128 : (kc + 1) * 128],
                                         xn_tok[:, kc * 128 : (kc + 1) * 128], identb[:]))
                bf16t_done[tt] = TE.n
                TE.after(ve)
                V.tick(ve.tensor_copy(
                    xnT[:, 0 : 8 * T].rearrange("p (k t) -> p k t", t=T)[:, :, tt * 128 : (tt + 1) * 128],
                    pt[:].rearrange("p (k t) -> p k t", t=128)))
                evac_done[b] = V.n
                V.after(sy, st1_done[tt])
                XND[b].tick(sy.dma_start(xnc_d[tt * 128 : (tt + 1) * 128, 0:1024], xn_tok))
                if tt + 2 < NT:
                    V.after(sy, sub_done[tt])
                    X[b].tick(sy.dma_start(xt[:, b * 1024 : (b + 1) * 1024],
                                           x_d[(tt + 2) * 128 : (tt + 3) * 128, :]))
                    xload.append(X[b].n)
                V.after(te, xnf_ev)
                for kc in range(8):
                    TE.tick(te.transpose(ptf[:, kc * 128 : (kc + 1) * 128],
                                         scrv[b][:, kc * 128 : (kc + 1) * 128], identf[:]))
                f32t_done[tt] = TE.n
                TE.after(se)
                S.tick(se.copy(outf[:], ptf[:]))
                xnf_ev_s = S.n
                S.after(te, xnf_ev_s)
                V.after(te, gate_read)
                for kc in range(8):
                    TE.tick(te.matmul(pga[:],
                                      outf[:, kc * 128 : (kc + 1) * 128],
                                      gateTf[:, kc * 8 : (kc + 1) * 8],
                                      start=(kc == 0), stop=(kc == 7)))
                TE.after(ve)
                V.tick(ve.tensor_copy(cbig[:, tt * 8 : (tt + 1) * 8], pga[:]))
                gate_read = V.n
                xnf_ev = 0
                return xnf_ev_s

            s_ev = 0
            for tt in range(NT):
                stage1(tt)
                if tt >= 1:
                    if s_ev:
                        S.after(te, s_ev)
                    s_ev = stage2(tt - 1)
            if s_ev:
                S.after(te, s_ev)
            stage2(NT - 1)

        NT8 = NT * 8
        c3 = cbig[:, 0:NT8].rearrange("p (t e) -> p t e", e=8)
        V.tick(ve.reduce_max(stat[:, 32 : 32 + NT], c3, axis=AX.X))
        V.tick(ve.tensor_tensor(scr[:, 0:NT8].rearrange("p (t e) -> p t e", e=8), c3,
                                stat[:, 32 : 32 + NT].unsqueeze(2).broadcast_to([128, NT, 8]),
                                OP.subtract))
        V.after(se)
        S.tick(se.activation(scr[:, 256 : 256 + NT8], scr[:, 0:NT8], AF.Exp))
        z = scr[:, 256 : 256 + NT8]
        z3 = z.rearrange("p (t e) -> p t e", e=8)
        m1 = scr[:, 512 : 512 + NT8]
        S.after(ve)
        V.tick(ve.reduce_max(stat[:, 32 : 32 + NT], z3, axis=AX.X))
        V.tick(ve.tensor_tensor(m1.rearrange("p (t e) -> p t e", e=8), z3,
                                stat[:, 32 : 32 + NT].unsqueeze(2).broadcast_to([128, NT, 8]),
                                OP.is_ge))
        V.tick(ve.scalar_tensor_tensor(scr[:, 0:NT8], m1, -1e30, z, OP.mult, OP.add))
        V.tick(ve.reduce_max(stat[:, 48 : 48 + NT],
                             scr[:, 0:NT8].rearrange("p (t e) -> p t e", e=8), axis=AX.X))
        V.tick(ve.tensor_tensor(msk[:, 0:NT8].rearrange("p (t e) -> p t e", e=8), z3,
                                stat[:, 48 : 48 + NT].unsqueeze(2).broadcast_to([128, NT, 8]),
                                OP.is_ge))
        V.tick(ve.tensor_mul(scr[:, 0:NT8], msk[:, 0:NT8], z))
        zk3 = scr[:, 0:NT8].rearrange("p (t e) -> p t e", e=8)
        V.tick(ve.reduce_sum(stat[:, 32 : 32 + NT], zk3, axis=AX.X))
        V.tick(ve.reciprocal(stat[:, 48 : 48 + NT], stat[:, 32 : 32 + NT]))
        V.tick(ve.tensor_tensor(c3, zk3,
                                stat[:, 48 : 48 + NT].unsqueeze(2).broadcast_to([128, NT, 8]),
                                OP.mult))
        coeff_done = V.n
        V.after(sy, coeff_done)
        for tt in range(NT):
            CD.tick(sy.dma_start(xnc_f[tt * 128 : (tt + 1) * 128, 512:520],
                                 cbig[:, tt * 8 : (tt + 1) * 8]))

        if "xn" in dbg:
            V.after(gp)
            DBG.tick(gp.dma_start(dbg_d["xn"], xnT[:, 0 : 8 * T]))
        if "gate" in dbg:
            V.after(gp)
            DBG.tick(gp.dma_start(dbg_d["gate"], cbig[:, 0 : NT * 8]))

        if upto == "ln":
            V.after(sy)
            OUT.tick(sy.dma_start(out_d[0:128, :], scr[:]))
            OUT.after(sy)
            DBG.after(sy)
            XND[0].after(sy)
            XND[1].after(sy)
            CD.after(sy)
            GZ.after(sy)
            return nc

        IDXI = Ctr(_sem("smIDXI"), 16)
        SCI = Ctr(_sem("smSCI"), 16)
        IDXL = Ctr(_sem("smIDXL"), 16)
        NSLOTF = float(NSLOT)
        with nc.psum_tensor("prk", [128, 128], F32) as prk, \
             nc.psum_tensor("ptrk", [128, 128], F32) as ptr, \
             nc.psum_tensor("pbr", [128, 128], F32) as pbr:
            CQ.after(te)
            mask_v = V.n
            V.after(te, mask_v)
            TE.tick(te.matmul(prk[:, 0:NT8], ltri_s[:], msk[:, 0:NT8],
                              start=True, stop=True))
            TE.after(ve)
            wsb = scr2
            V.tick(ve.tensor_copy(wsb[:, 0:NT8], prk[:, 0:NT8]))
            winv = V.n
            V.after(te, winv)
            TE.tick(te.transpose(ptr[0:NT8, 0:128], wsb[:, 0:NT8], identf[:]))
            TE.after(ve)
            V.tick(ve.memset(gsc[:, 0:1], 0.0))
            V.tick(ve.tensor_copy(gsc[0:NT8, 0:1], ptr[0:NT8, 127:128]))
            cnt_v = V.n
            V.after(te, cnt_v)
            TE.tick(te.matmul(pbr[:, 0:1], pfx_s[:], gsc[:, 0:1], start=True, stop=True))
            TE.after(ve)
            V.tick(ve.tensor_copy(gsc[:, 1:2], pbr[:, 0:1]))
            off_v = V.n
            V.after(te, off_v)
            TE.tick(te.transpose(ptr[0:1, 0:128], gsc[:, 1:2], identf[:]))
            TE.after(ve)
            V.tick(ve.tensor_copy(scr[0:1, 768:896], ptr[0:1, 0:128]))
            offr_v = V.n
            V.after(te, offr_v)
            TE.tick(te.matmul(pbr[:, 0:128], onesq_s[0:1, 0:128], scr[0:1, 768:896],
                              start=True, stop=True))
            TE.after(ve)
            V.tick(ve.tensor_tensor(rall[:, 0:NT8], wsb[:, 0:NT8], pbr[:, 0:NT8], OP.add))
            m2 = scr[:, 0:NT8]
            V.tick(ve.tensor_sub(m2, msk[:, 0:NT8], m1))
            radj = scr[:, 256 : 256 + NT8]
            V.tick(ve.tensor_tensor(radj.rearrange("p (t e) -> p t e", e=8),
                                    rall[:, 0:NT8].rearrange("p (t e) -> p t e", e=8),
                                    ecap_sb[:].unsqueeze(1).broadcast_to([128, NT, 8]),
                                    OP.add))
            sidx3 = sidx[:, 0 : NT * 2].rearrange("p (t j) -> p t j", j=2)
            for j, mj in ((0, m1), (1, m2)):
                V.tick(ve.tensor_mul(scr[:, 128 : 128 + NT8], mj, radj))
                V.tick(ve.reduce_sum(stat[:, 32 : 32 + NT],
                                     scr[:, 128 : 128 + NT8].rearrange("p (t e) -> p t e", e=8),
                                     axis=AX.X))
                V.tick(ve.tensor_mul(scr[:, 128 : 128 + NT8], mj, rall[:, 0:NT8]))
                V.tick(ve.reduce_sum(stat[:, 48 : 48 + NT],
                                     scr[:, 128 : 128 + NT8].rearrange("p (t e) -> p t e", e=8),
                                     axis=AX.X))
                V.tick(ve.tensor_scalar(stat[:, 0:NT], stat[:, 48 : 48 + NT],
                                        float(CAP), None, OP.is_gt))
                V.tick(ve.scalar_tensor_tensor(stat[:, 48 : 48 + NT], stat[:, 0:NT],
                                               NSLOTF, stat[:, 32 : 32 + NT],
                                               OP.mult, OP.add))
                V.tick(ve.tensor_scalar(sidx3[:, :, j : j + 1],
                                        stat[:, 48 : 48 + NT].unsqueeze(2),
                                        NSLOTF, None, OP.min))
            sidx_v = V.n
        V.after(gp, sidx_v)
        IDXI.tick(gp.dma_start(idx_d[0:NSLOT].rearrange("(p a) -> p a", a=NSLOT // 128), ibig[:]))
        IDXI.after(gp)
        for tt in range(NT):
            for j in range(2):
                SCI.tick(gp.indirect_dma_start(
                    idx_d.rearrange("(a b) -> a b", b=1),
                    IndirectOffsetOnAxis(sidx[:, tt * 2 + j : tt * 2 + j + 1], 0),
                    tokid_sb[:, tt : tt + 1], None,
                    oob_is_err=False))
        SCI.after(gp)
        IDXL.tick(gp.dma_start(idxstg[:], idx_d[0:NSLOT].rearrange("(a b) -> a b", b=128)))
        NA = NSLOT // 128
        with nc.psum_tensor("ptx", [128, 64], F32) as ptx:
            IDXL.after(ve)
            V.tick(ve.tensor_copy(idxstf[:], idxstg[:]))
            V.after(te)
            TE.tick(te.transpose(ptx[:, 0:NA], idxstf[:], identf[0:NA, 0:NA]))
            TE.after(ve)
            V.tick(ve.tensor_copy(idxf2[:], ptx[:, 0:NA]))
            V.after(ve)
            V.tick(ve.tensor_copy(idx_sb[:], idxf2[:]))
        idx_ready = V.n

        HH0 = 0
        WST = NHC * CAP
        w1t_o = [WST, WST + 1024, WST + 2048, WST + 3072]
        w3t_o = [WST + 4096, WST + 5120, WST + 6144, WST + 7168]
        W2T = WST + 8192
        GX0 = W2T + NHC * 1024
        GXT0 = GX0 + NCT * 1040
        SC0 = GXT0 + 2 * 8 * CAP
        assert SC0 + 8192 <= AR
        assert GX0 % 2 == 0 and SC0 % 4 == 0

        n_tq = chunks(CAP, 512)
        with nc.psum_tensor("mt0", [128, 1024], BF16) as mt0, \
             nc.psum_tensor("mt1", [128, 1024], BF16) as mt1, \
             nc.psum_tensor("h1a", [128, 512], F32) as h1a, \
             nc.psum_tensor("h3a", [128, 512], F32) as h3a, \
             nc.psum_tensor("h1b", [128, 512], F32) as h1b, \
             nc.psum_tensor("h3b", [128, 512], F32) as h3b:
            wload = {}
            stage_free = {}
            GX = Ctr(_sem("smGX"), 16)
            SCAT = [Ctr(_sem("smSCA"), 16), Ctr(_sem("smSCB"), 16)]
            scat_done_prev = [0, 0]
            W2C = Ctr(_sem("smW2C"), 16)
            w2ld = {}
            _w2ch = [(0, 3), (3, 6), (6, 9), (9, 12), (12, 15), (15, 18), (18, 20), (20, 22)]

            def issue_w13(e, hc):
                if e >= ne:
                    return
                slot = (e * NHC + hc) % 4
                key = ("s13", slot)
                if key in stage_free:
                    TE.after(gp, stage_free[key])
                W13[slot].tick(gp.dma_start(
                    arena[:, w1t_o[slot] : w1t_o[slot] + 1024].rearrange("p (k h) -> p k h", h=128),
                    w1t_d[e].rearrange("(k p) h -> p k h", p=128)[:, :, hc * 128 : (hc + 1) * 128]))
                W13[slot].tick(gp.dma_start(
                    arena[:, w3t_o[slot] : w3t_o[slot] + 1024].rearrange("p (k h) -> p k h", h=128),
                    w3t_d[e].rearrange("(k p) h -> p k h", p=128)[:, :, hc * 128 : (hc + 1) * 128]))
                wload[("w13", e, hc)] = (W13[slot], W13[slot].n)

            def issue_w2(e, j):
                if e >= ne:
                    return
                hc0, hc1 = _w2ch[j]
                TE.after(gp, w2t_free)
                W2C.tick(gp.dma_start(
                    arena[:, W2T + hc0 * 1024 : W2T + hc1 * 1024].rearrange("p (h c) -> p h c", c=1024),
                    w2t_d[e].rearrange("(h p) c -> p h c", p=128)[:, hc0:hc1, :]))
                w2ld[(e, j)] = W2C.n

            WKVT = AR - 4096
            for _kc in range(8):
                STG.tick(gp.dma_start(arena[:, WKVT + _kc * 512 : WKVT + (_kc + 1) * 512],
                                      wkvt_d[_kc * 128 : (_kc + 1) * 128, :]))
            stg_kv = STG.n
            for _hc0 in range(4):
                issue_w13(0, _hc0)
            V.after(te)
            S.after(te)

            mtev = [0, 0]
            hfree = [0, 0]
            hsil = [0, 0]
            w2t_free = 0
            gx_te_free = 0
            gx_ve_free = 0
            gxt_free = [0, 0]
            gxt_vready = [0, 0]
            last_hh = [0]

            stages = []
            for e in range(ne):
                stages.append(("G", e, 0))
                for hc in range(NHC):
                    stages.append(("A", e, hc))
                stages.append(("B", e, 0))

            def prep(st):
                nonlocal gx_te_free, gx_ve_free
                kind, e, x = st
                if kind == "G":
                    gslot = e % 2
                    gxt = GXT0 + gslot * 8 * CAP
                    TE.after(gp, gx_te_free)
                    V.after(gp, max(gx_ve_free, idx_ready))
                    GZ.after(gp)
                    for tt2 in range(NT):
                        XND[tt2 % 2].after(gp)
                    CD.after(gp)
                    for stt in range(NCT):
                        GX.tick(gp.indirect_dma_start(
                            arena[:, GX0 + stt * 1040 : GX0 + (stt + 1) * 1040], None,
                            xnc_d, IndirectOffsetOnAxis(idx_sb[:, e * NCT + stt : e * NCT + stt + 1], 0),
                            oob_is_err=False))
                    GX.after(te)
                    for stt in range(NCT):
                        slot = stt % 2
                        mtb = mt0 if slot == 0 else mt1
                        V.after(te, mtev[slot])
                        for kc in range(8):
                            TE.tick(te.transpose(mtb[:, kc * 128 : (kc + 1) * 128],
                                                 arena[:, GX0 + stt * 1040 + kc * 128 : GX0 + stt * 1040 + (kc + 1) * 128],
                                                 identb[:]))
                        TE.after(ve)
                        if gxt_free[gslot]:
                            TE.after(ve, gxt_free[gslot])
                        V.tick(ve.tensor_copy(
                            arena[:, gxt : gxt + 8 * CAP].rearrange("p (k t) -> p k t", t=CAP)
                                 [:, :, stt * 128 : (stt + 1) * 128],
                            mtb[:].rearrange("p (k t) -> p k t", t=128)))
                        mtev[slot] = V.n
                    gx_te_free = TE.n
                    gxt_vready[gslot] = V.n
                    gxf = arena[:, GX0 : GX0 + NCT * 1040].bitcast(F32)
                    for stt in range(NCT):
                        V.after(ve)
                        V.tick(ve.tensor_copy(cg_sb[:, gslot * NCT + stt : gslot * NCT + stt + 1],
                                              gxf[:, stt * 520 + 512 + e : stt * 520 + 512 + e + 1]))
                    gx_ve_free = V.n
                    return {"gslot": gslot, "gxt": gxt, "cg_v": V.n}
                elif kind == "A":
                    hc = x
                    return {"slot": (e * NHC + hc) % 4, "wt": wload[("w13", e, hc)]}
                else:
                    return {"w2t_ready": w2ld[(e, len(_w2ch) - 1)]}

            def compute(st, pr):
                nonlocal w2t_free
                kind, e, x = st
                if kind == "G":
                    return
                if kind == "A":
                    hc = x
                    slot = pr["slot"]
                    _wc, _wv = pr["wt"]
                    _wc.after(te, _wv)
                    V.after(te, gxt_vready[e % 2])
                    gxt = GXT0 + (e % 2) * 8 * CAP
                    for qi, (toff, tlen) in enumerate(n_tq):
                        hb = qi % 2
                        h1 = h1a if hb == 0 else h1b
                        h3 = h3a if hb == 0 else h3b
                        V.after(te, hfree[hb])
                        S.after(te, hsil[hb])
                        for kc in range(8):
                            TE.tick(te.matmul(h1[:, 0:tlen],
                                              arena[:, w1t_o[slot] + kc * 128 : w1t_o[slot] + (kc + 1) * 128],
                                              arena[:, gxt + kc * CAP + toff : gxt + kc * CAP + toff + tlen],
                                              start=(kc == 0), stop=(kc == 7)))
                        h1_done = TE.n
                        for kc in range(8):
                            TE.tick(te.matmul(h3[:, 0:tlen],
                                              arena[:, w3t_o[slot] + kc * 128 : w3t_o[slot] + (kc + 1) * 128],
                                              arena[:, gxt + kc * CAP + toff : gxt + kc * CAP + toff + tlen],
                                              start=(kc == 0), stop=(kc == 7)))
                        h3_done = TE.n
                        if hc == NHC - 1:
                            gxt_free[e % 2] = TE.n
                        if qi == len(n_tq) - 1:
                            stage_free[("s13", slot)] = TE.n
                            if hc + 4 < NHC:
                                issue_w13(e, hc + 4)
                            else:
                                issue_w13(e + 1, hc + 4 - NHC)
                            if 3 <= hc <= 10:
                                issue_w2(e, hc - 3)
                        TE.after(se, h1_done)
                        sil = expb[:, 1024 + hb * 512 : 1024 + hb * 512 + 512]
                        S.tick(se.activation(sil[:, 0:tlen], h1[:, 0:tlen], AF.Silu))
                        hsil[hb] = S.n
                        S.after(ve, hsil[hb])
                        TE.after(ve, h3_done)
                        V.tick(ve.tensor_tensor(
                            arena[:, HH0 + hc * CAP + toff : HH0 + hc * CAP + toff + tlen],
                            sil[:, 0:tlen], h3[:, 0:tlen], OP.mult))
                        hfree[hb] = V.n
                        last_hh[0] = V.n
                else:
                    gslot = e % 2
                    scf = arena[:, SC0 : SC0 + 8192].bitcast(F32)
                    for ti in range(NCT):
                        ssl = ti % 2
                        sc_v = scf[:, ssl * 2048 : ssl * 2048 + 1024]
                        SCAT[ssl].after(ve)
                        for ch in range(2):
                            ob = ch
                            opsum = h1a if ob == 0 else h1b
                            V.after(te, hfree[ob])
                            S.after(te, hsil[ob])
                            W2C.after(te, pr["w2t_ready"])
                            V.after(te, last_hh[0])
                            for hc in range(NHC):
                                TE.tick(te.matmul(opsum[:, 0:512],
                                                  arena[:, HH0 + hc * CAP + ti * 128 : HH0 + hc * CAP + (ti + 1) * 128],
                                                  arena[:, W2T + hc * 1024 + ch * 512 : W2T + hc * 1024 + ch * 512 + 512],
                                                  start=(hc == 0), stop=(hc == NHC - 1)))
                            o_done = TE.n
                            TE.after(ve, o_done)
                            V.tick(ve.tensor_scalar(sc_v[:, ch * 512 : ch * 512 + 512], opsum[:, 0:512],
                                                    cg_sb[:, gslot * NCT + ti : gslot * NCT + ti + 1],
                                                    None, OP.mult))
                            hfree[ob] = V.n
                        sc_ready = V.n
                        V.after(gp, sc_ready)
                        GZ.after(gp)
                        SCAT[0].after(gp)
                        SCAT[1].after(gp)
                        SCAT[ssl].tick(gp.indirect_dma_start(
                            g_d, IndirectOffsetOnAxis(idx_sb[:, e * NCT + ti : e * NCT + ti + 1], 0),
                            sc_v, None,
                            oob_is_err=False, compute_op=OP.add))
                    w2t_free = TE.n
                    scat_done_prev[0] = SCAT[0].n
                    scat_done_prev[1] = SCAT[1].n

            if stages:
                pr_prev = prep(stages[0])
                for i in range(1, len(stages)):
                    pr_next = prep(stages[i])
                    compute(stages[i - 1], pr_prev)
                    pr_prev = pr_next
                compute(stages[-1], pr_prev)
            SCAT[0].after(sy)
            SCAT[1].after(sy)

        if "gmoe" in dbg:
            SCAT[0].after(sy)
            SCAT[1].after(sy)
            DBG.tick(sy.dma_start(dbg_d["gmoe"], g_d[0:T, :]))

        if upto == "moe":
            V.after(sy)
            OUT.tick(sy.dma_start(out_d[0:128, :], scr[:]))
            OUT.after(sy)
            DBG.after(sy)
            return nc

        QT0 = 0
        NQT = 12
        KT0 = NQT * T
        VA0 = KT0 + 8 * T
        WQT = VA0 + NT * 260
        WOUTT = WQT + 12288
        assert WOUTT + 8192 <= AR
        ST0 = QT0 if 12 * T >= 8192 else AR - 8192

        _kpos = {0: 0, 3: 1, 6: 2, 1: 3, 4: 4, 7: 5, 2: 6, 5: 7}

        def kslot(g, s):
            sl = 2 * g + s
            return KT0 + _kpos[sl] * T, (sl % 3) * 32

        _qmap = {}
        _base_fill = [0, 0, 0]
        for _h in range(HEADS):
            for _s in range(2):
                _b = (2 * (_h // 4) + _s) % 3
                _qmap[(_h, _s)] = (_base_fill[_b], _b * 32)
                _base_fill[_b] += 1
        assert max(_base_fill) <= NQT

        def qslot(h, s):
            tile, base = _qmap[(h, s)]
            return QT0 + tile * T, base

        _qbase_pieces = {0: [], 1: [], 2: []}
        for _h in range(HEADS):
            for _s in range(2):
                _b = (2 * (_h // 4) + _s) % 3
                _qbase_pieces[_b].append((_h, _s))
        assert [len(_qbase_pieces[b]) for b in range(3)] == [12, 12, 8]

        TE.after(gp)
        V.after(gp)
        S.after(gp)
        for kc in range(8):
            SQT.tick(gp.dma_start(arena[:, WQT + kc * 1024 : WQT + (kc + 1) * 1024],
                                  wqt_d[kc * 128 : (kc + 1) * 128, :]))
        for kc in range(8):
            SOT.tick(gp.dma_start(arena[:, WOUTT + kc * 1024 : WOUTT + (kc + 1) * 1024],
                                  woutt_d[kc * 128 : (kc + 1) * 128, :]))

        with nc.psum_tensor("kvA", [128, 512], F32) as kvA, \
             nc.psum_tensor("q1A", [128, 512], F32) as q1A, \
             nc.psum_tensor("q2A", [128, 512], F32) as q2A, \
             nc.psum_tensor("kvB", [128, 512], F32) as kvB, \
             nc.psum_tensor("q1B", [128, 512], F32) as q1B, \
             nc.psum_tensor("q2B", [128, 512], F32) as q2B, \
             nc.psum_tensor("ptrV", [128, 1024], BF16) as ptrV, \
             nc.psum_tensor("ptrS", [128, 1024], BF16) as ptrS:
            psets = [(kvA, q1A, q2A), (kvB, q1B, q2B)]
            TE.after(ve)
            V.tick(ve.memset(arena[:, QT0 : KT0 + 8 * T], 0.0))
            zq_v = V.n
            vread_set = [0, 0]
            kv_done = {}
            q_done = {}
            pv_free = 0
            ps_free = 0
            s_memset_gate = [False]

            def stage_proj(tt):
                s = tt % 2
                kvp, q1p, q2p = psets[s]
                if tt == 0:
                    STG.after(te, stg_kv)
                V.after(te, vread_set[s])
                for kc in range(8):
                    TE.tick(te.matmul(kvp[:, 0:512],
                                      xnT[:, kc * T + tt * 128 : kc * T + tt * 128 + 128],
                                      arena[:, WKVT + kc * 512 : WKVT + (kc + 1) * 512],
                                      start=(kc == 0), stop=(kc == 7)))
                kv_done[tt] = TE.n
                if tt == 0:
                    SQT.after(te)
                for h2 in range(2):
                    pq = q1p if h2 == 0 else q2p
                    for kc in range(8):
                        TE.tick(te.matmul(pq[:, 0:512],
                                          xnT[:, kc * T + tt * 128 : kc * T + tt * 128 + 128],
                                          arena[:, WQT + kc * 1024 + h2 * 512 : WQT + kc * 1024 + h2 * 512 + 512],
                                          start=(kc == 0), stop=(kc == 7)))
                q_done[tt] = TE.n

            def stage_rope(tt):
                nonlocal pv_free, ps_free
                s = tt % 2
                kvp, q1p, q2p = psets[s]
                TE.after(ve, kv_done[tt])
                V.tick(ve.memset(arena[:, VA0 + tt * 260 : VA0 + (tt + 1) * 260], 1.0))
                V.tick(ve.tensor_copy(
                    arena[:, VA0 + tt * 260 : VA0 + (tt + 1) * 260].rearrange("p (g d) -> p g d", d=65)[:, :, 0:64],
                    kvp[:, 256:512].rearrange("p (g d) -> p g d", d=64)))
                kx1 = kvp[:, 0:256].rearrange("p (a j two) -> p a j two", two=2, j=16)[:, :, :, 0:1].squeeze(3)
                kx2 = kvp[:, 0:256].rearrange("p (a j two) -> p a j two", two=2, j=16)[:, :, :, 1:2].squeeze(3)
                cs8 = cosb[:, tt * 16 : (tt + 1) * 16].unsqueeze(1).broadcast_to([128, 8, 16])
                sn8 = sinb[:, tt * 16 : (tt + 1) * 16].unsqueeze(1).broadcast_to([128, 8, 16])
                V.tick(ve.tensor_tensor(scr[:, 0:128].rearrange("p (a j) -> p a j", j=16), kx1, cs8, OP.mult))
                V.tick(ve.tensor_tensor(scr[:, 128:256].rearrange("p (a j) -> p a j", j=16), kx2, sn8, OP.mult))
                V.tick(ve.tensor_tensor(scr[:, 256:384].rearrange("p (a j) -> p a j", j=16), kx1, sn8, OP.mult))
                V.tick(ve.tensor_tensor(scr[:, 384:512].rearrange("p (a j) -> p a j", j=16), kx2, cs8, OP.mult))
                kro = expb[:, 0:256]
                V.tick(ve.tensor_tensor(
                    kro.rearrange("p (a two) -> p a two", two=2)[:, :, 0:1].squeeze(2),
                    scr[:, 0:128], scr[:, 128:256], OP.subtract))
                V.tick(ve.tensor_tensor(
                    kro.rearrange("p (a two) -> p a two", two=2)[:, :, 1:2].squeeze(2),
                    scr[:, 256:384], scr[:, 384:512], OP.add))
                krope_done = V.n
                V.after(te, krope_done)
                V.after(te, pv_free)
                _kgroups = [(0, [0, 3, 6]), (32, [1, 4, 7]), (64, [2, 5])]
                _col = 0
                _kcols = {}
                for _base, _sls in _kgroups:
                    for _sl in _sls:
                        _g, _s2 = _sl // 2, _sl % 2
                        TE.tick(te.transpose(ptrV[_base : _base + 32, _col * 128 : _col * 128 + 128],
                                             kro[:, _g * 64 + _s2 * 32 : _g * 64 + _s2 * 32 + 32], identb[:]))
                        _col += 1
                ktr = TE.n
                TE.after(ve, ktr)
                _col = 0
                for _base, _sls in _kgroups:
                    _n = len(_sls)
                    _t0 = _kpos[_sls[0]]
                    V.tick(ve.tensor_copy(
                        arena[_base : _base + 32, KT0 + _t0 * T : KT0 + (_t0 + _n) * T]
                            .rearrange("p (a c) -> p a c", c=T)[:, :, tt * 128 : (tt + 1) * 128],
                        ptrV[_base : _base + 32, _col * 128 : (_col + _n) * 128]
                            .rearrange("p (a c) -> p a c", c=128)))
                    _col += _n
                k_pv, k_ps = V.n, S.n
                TE.after(ve, q_done[tt])
                qro = expb[:, 0:1024]
                c16 = cosb[:, tt * 16 : (tt + 1) * 16].unsqueeze(1).broadcast_to([128, 16, 16])
                s16 = sinb[:, tt * 16 : (tt + 1) * 16].unsqueeze(1).broadcast_to([128, 16, 16])
                for hb, pq in enumerate([q1p, q2p]):
                    qx1 = pq[:, 0:512].rearrange("p (a j two) -> p a j two", two=2, j=16)[:, :, :, 0:1].squeeze(3)
                    qx2 = pq[:, 0:512].rearrange("p (a j two) -> p a j two", two=2, j=16)[:, :, :, 1:2].squeeze(3)
                    V.tick(ve.tensor_tensor(scr[:, 0:256].rearrange("p (a j) -> p a j", j=16), qx1, c16, OP.mult))
                    V.tick(ve.tensor_tensor(scr[:, 256:512].rearrange("p (a j) -> p a j", j=16), qx2, s16, OP.mult))
                    V.tick(ve.tensor_tensor(scr[:, 512:768].rearrange("p (a j) -> p a j", j=16), qx1, s16, OP.mult))
                    V.tick(ve.tensor_tensor(scr[:, 768:1024].rearrange("p (a j) -> p a j", j=16), qx2, c16, OP.mult))
                    qd = qro[:, hb * 512 : (hb + 1) * 512].rearrange("p (a two) -> p a two", two=2)
                    V.tick(ve.tensor_tensor(qd[:, :, 0:1].squeeze(2), scr[:, 0:256], scr[:, 256:512], OP.subtract))
                    V.tick(ve.tensor_tensor(qd[:, :, 1:2].squeeze(2), scr[:, 512:768], scr[:, 768:1024], OP.add))
                vread_set[s] = V.n
                qrope_done = V.n
                V.after(te, qrope_done)
                V.after(te, k_pv)
                S.after(te, ps_free)
                if not s_memset_gate[0]:
                    V.after(se, zq_v)
                    s_memset_gate[0] = True
                def _qround(chunks):
                    cols = {id(ptrV): 0, id(ptrS): 0}
                    plan = []
                    for bank, base, t0, pieces in chunks:
                        c0 = cols[id(bank)]
                        for j, (h, s2) in enumerate(pieces):
                            TE.tick(te.transpose(
                                bank[base : base + 32, (c0 + j) * 128 : (c0 + j) * 128 + 128],
                                qro[:, h * 64 + s2 * 32 : h * 64 + s2 * 32 + 32], identb[:]))
                        cols[id(bank)] = c0 + len(pieces)
                        plan.append((bank, base, t0, c0, len(pieces)))
                    qtr = TE.n
                    TE.after(ve, qtr)
                    TE.after(se, qtr)
                    for bank, base, t0, c0, n in plan:
                        dst = arena[base : base + 32, QT0 + t0 * T : QT0 + (t0 + n) * T] \
                            .rearrange("p (a c) -> p a c", c=T)[:, :, tt * 128 : (tt + 1) * 128]
                        srcap = bank[base : base + 32, c0 * 128 : (c0 + n) * 128] \
                            .rearrange("p (a c) -> p a c", c=128)
                        if bank is ptrV:
                            V.tick(ve.tensor_copy(dst, srcap))
                        else:
                            S.tick(se.copy(dst, srcap))
                b0, b1, b2 = _qbase_pieces[0], _qbase_pieces[1], _qbase_pieces[2]
                _qround([(ptrV, 0, 0, b0[0:8]), (ptrS, 32, 0, b1[0:8])])
                V.after(te, V.n)
                S.after(te, S.n)
                _qround([(ptrV, 0, 8, b0[8:12]), (ptrV, 64, 0, b2[0:4]),
                         (ptrS, 32, 8, b1[8:12]), (ptrS, 64, 4, b2[4:8])])
                pv_free, ps_free = V.n, S.n

            for tt in range(NT):
                stage_proj(tt)
                if tt >= 1:
                    stage_rope(tt - 1)
            stage_rope(NT - 1)

        if upto == "proj":
            V.after(sy)
            OUT.tick(sy.dma_start(out_d[0:128, :], scr[:]))
            OUT.after(sy)
            DBG.after(sy)
            return nc

        y_all = xnT
        with nc.psum_tensor("sA", [128, 1024], F32) as sA, \
             nc.psum_tensor("sB", [128, 1024], F32) as sB, \
             nc.psum_tensor("sC", [128, 1024], F32) as sC, \
             nc.psum_tensor("uA", [128, 260], F32) as uA, \
             nc.psum_tensor("uB", [128, 260], F32) as uB:
            s_banks = [sA, sB, sC]
            u_sets = [(uA, uB)]
            V.after(te)
            S.after(te)
            exp_done = [0, 0, 0]
            mask_done = [0, 0, 0]
            av_done = [0, 0, 0]
            uev_set = [0]
            pend_cmb = [None]

            def emit_combine():
                if pend_cmb[0] is None:
                    return
                ch0, ch1, cqb, cua, cub, csi, cavn = pend_cmb[0]
                pend_cmb[0] = None
                TE.after(ve, cavn)
                V.tick(ve.tensor_copy(scr[:, 0:260], cua[:, 0:260]))
                V.tick(ve.tensor_copy(scr[:, 512:772], cub[:, 0:260]))
                uev_set[csi] = V.n
                V.tick(ve.reciprocal(stat[:, 16:20],
                                     scr[:, 0:260].rearrange("p (a d) -> p a d", d=65)[:, :, 64:65].squeeze(2)))
                V.tick(ve.reciprocal(stat[:, 20:24],
                                     scr[:, 512:772].rearrange("p (a d) -> p a d", d=65)[:, :, 64:65].squeeze(2)))
                for hi, h in enumerate([ch0, ch1]):
                    ub = 0 if hi == 0 else 512
                    rbase = 16 + hi * 4
                    for qt in range(2):
                        col1 = ub + (0 * 2 + qt) * 65
                        col2 = ub + (1 * 2 + qt) * 65
                        tt = cqb * 2 + qt
                        r1c = stat[:, rbase + qt : rbase + qt + 1]
                        r2c = stat[:, rbase + 2 + qt : rbase + 2 + qt + 1]
                        V.tick(ve.tensor_scalar(scr[:, 300:364], scr[:, col1 : col1 + 64], r1c, None, OP.mult))
                        V.tick(ve.tensor_scalar(scr[:, 364:428], scr[:, col2 : col2 + 64], r2c,
                                                lamb[:, h : h + 1], OP.mult, OP.mult))
                        V.tick(ve.tensor_tensor(y_all[:, tt * 1024 + h * 64 : tt * 1024 + h * 64 + 64],
                                                scr[:, 300:364], scr[:, 364:428], OP.subtract))

            qbi = 0
            for p in range(HEADS // 2):
                h0, h1_ = 2 * p, 2 * p + 1
                g = p // 2
                for qb in range(NQB):
                    kc_max = min(2 * (qb + 1), NT)
                    si = 0
                    ua_, ub_ = u_sets[si]
                    qbi += 1

                    def emit_uzero():
                        V.after(te, uev_set[si])
                        TE.tick(te.matmul(ua_[:, 0:260], zrow[0:1, 0:128], zrow[0:1, 0:260],
                                          start=True, stop=False, skip_group_check=True))
                        TE.tick(te.matmul(ub_[:, 0:260], zrow[0:1, 0:128], zrow[0:1, 0:260],
                                          start=True, stop=False, skip_group_check=True))
                    pend = None

                    def emit_av(pk):
                        kc2, b2, ebm, msk = pk
                        if msk:
                            V.after(te, mask_done[b2])
                        else:
                            S.after(te, expd_at[b2])
                        for hi, h in enumerate([h0, h1_]):
                            bank = ua_ if hi == 0 else ub_
                            for s in range(2):
                                for qt in range(2):
                                    col = (s * 2 + qt) * 65
                                    last = (kc2 == kc_max - 1)
                                    mm = te.matmul(
                                        bank[:, col : col + 65],
                                        ebm[:, (hi * 2 + s) * 256 + qt * 128 : (hi * 2 + s) * 256 + qt * 128 + 128],
                                        arena[:, VA0 + kc2 * 260 + g * 65 : VA0 + kc2 * 260 + (g + 1) * 65],
                                        start=False, stop=last, skip_group_check=True)
                                    if hi == 1 and s == 1 and qt == 1:
                                        TE.tick(mm)
                        av_done[b2] = TE.n

                    expd_at = [0, 0, 0]
                    if qbi == 1:
                        emit_uzero()
                    for kc in range(kc_max):
                        b = kc % 3
                        sb_ = s_banks[b]
                        S.after(te, exp_done[b])
                        for ji, (h, s) in enumerate([(h0, 0), (h0, 1), (h1_, 0), (h1_, 1)]):
                            koff, kbase = kslot(h // 4, s)
                            qoff, qbase = qslot(h, s)
                            mm = te.matmul(sb_[:, ji * 256 : ji * 256 + 256],
                                           arena[:, koff + kc * 128 : koff + (kc + 1) * 128],
                                           arena[:, qoff + qb * 256 : qoff + (qb + 1) * 256],
                                           start=True, stop=True, skip_group_check=True)
                            if ji == 3:
                                TE.tick(mm)
                        sc_done = TE.n
                        TE.after(se, sc_done)
                        TE.after(se, av_done[b])
                        V.after(se, mask_done[b])
                        eb = expb[:, b * 1024 : (b + 1) * 1024]
                        S.tick(se.activation(eb, sb_[:], AF.Exp, scale=SCALE))
                        exp_done[b] = S.n
                        expd_at[b] = S.n
                        dd = kc - 2 * qb
                        masked = 0 <= dd <= 1
                        if masked:
                            S.after(ve, exp_done[b])
                            V.tick(ve.tensor_tensor(
                                eb.rearrange("p (a c) -> p a c", c=256),
                                eb.rearrange("p (a c) -> p a c", c=256),
                                maskb[:, dd * 256 : dd * 256 + 256].unsqueeze(1).broadcast_to([128, 4, 256]),
                                OP.mult))
                            mask_done[b] = V.n
                        if pend is not None:
                            emit_av(pend)
                        pend = (kc, b, eb, masked)
                        if kc == 0 and qbi > 1:
                            emit_combine()
                            emit_uzero()
                    if pend is not None:
                        emit_av(pend)
                    pend_cmb[0] = (h0, h1_, qb, ua_, ub_, si, TE.n)
            emit_combine()

        if upto == "attn":
            V.after(gp)
            DBG.tick(gp.dma_start(dbg_d["yatt"], y_all[:, 0 : 8 * T]))
            V.after(sy)
            OUT.tick(sy.dma_start(out_d[0:128, :], scr[:]))
            OUT.after(sy)
            DBG.after(sy)
            return nc

        if "yatt" in dbg:
            V.after(gp)
            DBG.tick(gp.dma_start(dbg_d["yatt"], y_all[:, 0 : 8 * T]))
            DBG.after(ve)

        with nc.psum_tensor("po0", [128, 512], F32) as po0, \
             nc.psum_tensor("po1", [128, 512], F32) as po1, \
             nc.psum_tensor("pty", [128, 1024], BF16) as pty, \
             nc.psum_tensor("pty2", [128, 1024], BF16) as pty2:
            xload2 = {}
            _o5 = NHC * CAP + ((NHC * CAP) & 1)
            xt = arena[:, _o5 : _o5 + 4096].bitcast(F32)
            outf = arena[:, _o5 + 4096 : _o5 + 6144].bitcast(F32)
            scr2 = arena[:, _o5 + 6144 : _o5 + 8192].bitcast(F32)
            ynTs2 = [arena[:, _o5 + 8192 : _o5 + 9216], arena[:, _o5 + 9216 : _o5 + 10240]]
            gread = arena[:, _o5 + 10240 : _o5 + 14336].bitcast(F32)
            GR = [Ctr(_sem("smGR0"), 16), Ctr(_sem("smGR1"), 16)]
            gload = {}
            SOT.after(te)
            V.after(sy)
            TE.after(sy)
            S.after(sy)
            SCAT[0].after(sy)
            SCAT[1].after(sy)
            TE.after(se)
            SCAT[0].after(se)
            SCAT[1].after(se)
            for tt in range(min(2, NT)):
                GR[tt % 2].tick(nc.scalar.dma_start(gread[:, (tt % 2) * 1024 : (tt % 2 + 1) * 1024],
                                                    g_d[tt * 128 : (tt + 1) * 128, :]))
                gload[tt] = GR[tt % 2].n
            for tt in range(min(2, NT)):
                X[tt % 2].tick(sy.dma_start(xt[:, (tt % 2) * 1024 : (tt % 2 + 1) * 1024],
                                            x_d[tt * 128 : (tt + 1) * 128, :]))
                xload2[tt] = X[tt % 2].n
            ptyev = 0
            pofree = [0, 0]
            scr2_read = [0, 0]
            st_prev = 0
            rmsst = arena[:, _o5 + 14336 : _o5 + 15360].bitcast(F32)
            for tt in range(NT):
                ysl = y_all[:, tt * 1024 : (tt + 1) * 1024]
                V.tick(ve.tensor_tensor(scr[:], ysl, ysl, OP.mult))
                V.tick(ve.reduce_sum(rmsst[:, tt * 16 : (tt + 1) * 16],
                                     scr[:].rearrange("p (h d) -> p h d", d=64), axis=AX.X))
            V.after(se)
            S.tick(se.activation(rmsst[:, 256 : 256 + 16 * NT], rmsst[:, 0 : 16 * NT],
                                 AF.Sqrt, bias=epsc[:, 1:2], scale=1.0 / HD))
            S.after(ve)
            V.tick(ve.reciprocal(rmsst[:, 0 : 16 * NT], rmsst[:, 256 : 256 + 16 * NT]))
            rms_v = V.n
            for tt in range(NT):
                ysl = y_all[:, tt * 1024 : (tt + 1) * 1024]
                V.tick(ve.tensor_tensor(ysl.rearrange("p (h d) -> p h d", d=64),
                                        ysl.rearrange("p (h d) -> p h d", d=64),
                                        rmsst[:, tt * 16 : (tt + 1) * 16].unsqueeze(2).broadcast_to([128, 16, 64]),
                                        OP.mult))
                V.tick(ve.tensor_tensor(ysl.rearrange("p (h d) -> p h d", d=64),
                                        ysl.rearrange("p (h d) -> p h d", d=64),
                                        rwb[:].unsqueeze(1).broadcast_to([128, 16, 64]), OP.mult))
            yn_done = V.n
            ptyev2 = [0, 0]
            ynT_done2 = [0, 0]
            ynT_read = [0, 0]

            def stage_t(tt):
                b = tt % 2
                ptyb = pty if b == 0 else pty2
                ysl = y_all[:, tt * 1024 : (tt + 1) * 1024]
                V.after(te, yn_done)
                V.after(te, ptyev2[b])
                for kc in range(8):
                    TE.tick(te.transpose(ptyb[:, kc * 128 : (kc + 1) * 128],
                                         ysl[:, kc * 128 : (kc + 1) * 128], identb[:]))
                TE.after(ve)
                if ynT_read[b]:
                    TE.after(ve, ynT_read[b])
                V.tick(ve.tensor_copy(ynTs2[b], ptyb[:]))
                ptyev2[b] = V.n
                ynT_done2[b] = V.n

            def stage_p(tt):
                nonlocal st_prev
                b = tt % 2
                for ch in range(2):
                    po = po0 if ch == 0 else po1
                    V.after(te, max(pofree[ch], ynT_done2[b]))
                    for kc in range(8):
                        TE.tick(te.matmul(po[:, 0:512], ynTs2[b][:, kc * 128 : (kc + 1) * 128],
                                          arena[:, WOUTT + kc * 1024 + ch * 512 : WOUTT + kc * 1024 + ch * 512 + 512],
                                          start=(kc == 0), stop=(kc == 7)))
                    s_done = TE.n
                    if ch == 1:
                        ynT_read[b] = TE.n
                    V.after(se, scr2_read[ch])
                    GR[tt % 2].after(se, gload[tt])
                    S.tick(se.activation(scr2[:, ch * 512 : ch * 512 + 512],
                                         gread[:, (tt % 2) * 1024 + ch * 512 : (tt % 2) * 1024 + ch * 512 + 512], AF.Silu))
                    sil_done = S.n
                    TE.after(ve, s_done)
                    S.after(ve, sil_done)
                    X[tt % 2].after(ve, xload2[tt])
                    OUT.after(ve, st_prev)
                    V.tick(ve.tensor_tensor(scr[:, ch * 512 : ch * 512 + 512], po[:, 0:512],
                                            scr2[:, ch * 512 : ch * 512 + 512], OP.mult))
                    scr2_read[ch] = V.n
                    V.tick(ve.tensor_tensor(outf[:, ch * 512 : ch * 512 + 512],
                                            scr[:, ch * 512 : ch * 512 + 512],
                                            xt[:, (tt % 2) * 1024 + ch * 512 : (tt % 2) * 1024 + ch * 512 + 512],
                                            OP.add))
                    pofree[ch] = V.n
                out_ready = V.n
                V.after(sy, out_ready)
                OUT.tick(sy.dma_start(out_d[tt * 128 : (tt + 1) * 128, :], outf[:]))
                st_prev = OUT.n
                if tt + 2 < NT:
                    V.after(sy, out_ready)
                    X[tt % 2].tick(sy.dma_start(xt[:, (tt % 2) * 1024 : (tt % 2 + 1) * 1024],
                                                x_d[(tt + 2) * 128 : (tt + 3) * 128, :]))
                    xload2[tt + 2] = X[tt % 2].n
                    S.after(se)
                    GR[tt % 2].tick(nc.scalar.dma_start(gread[:, (tt % 2) * 1024 : (tt % 2 + 1) * 1024],
                                                        g_d[(tt + 2) * 128 : (tt + 3) * 128, :]))
                    gload[tt + 2] = GR[tt % 2].n

            stage_t(0)
            for tt in range(NT):
                if tt + 1 < NT:
                    stage_t(tt + 1)
                stage_p(tt)
            OUT.after(sy)
            DBG.after(sy)

    return nc


_CONSTS = {}


def _consts(T):
    if T not in _CONSTS:
        inv = 1.0 / 10000.0 ** (np.arange(0, HALF, 2, dtype=np.float32) / HALF)
        t = np.arange(T, dtype=np.float32)
        fr = np.outer(t, inv)
        r = np.arange(128)[:, None]
        c = np.arange(256)[None, :]
        mask = np.zeros((128, 512), dtype=np.float32)
        mask[:, 0:256] = (c - r >= 0).astype(np.float32)
        mask[:, 256:512] = (c - r >= 128).astype(np.float32)
        CAP = cap_for(T)
        ecap = np.zeros((128, 8), dtype=np.float32)
        for e in range(8):
            ecap[:, e] = e * CAP - 1
        NT = T // 128
        pfx = np.zeros((128, 128), dtype=np.float32)
        for t2 in range(NT):
            for tt in range(NT):
                if t2 < tt:
                    for e in range(8):
                        pfx[t2 * 8 + e, tt * 8 + e] = 1.0
        _CONSTS[T] = dict(
            cosc=np.cos(fr).astype(np.float32),
            sinc=np.sin(fr).astype(np.float32),
            ident=np.eye(128, dtype=np.float32),
            maskc=mask,
            onesc=np.ones((1, 128), dtype=np.float32),
            ltri=np.triu(np.ones((128, 128), dtype=np.float32)),
            onesq=np.ones((128, 128), dtype=np.float32),
            tokid=np.arange(T, dtype=np.int32).reshape(T // 128, 128).T.copy(),
            ecap=ecap,
            pfx=pfx,
        )
    return _CONSTS[T]


def make_inputs(x, Wq, Wkv, Wout, lam, rmsw, gw, w1, w3, w2, T):
    consts = _consts(T)
    shared = dict(
        wqt=np.ascontiguousarray(np.asarray(Wq, dtype=np.float32).T),
        wkvt=np.ascontiguousarray(np.asarray(Wkv, dtype=np.float32).T),
        woutt=np.ascontiguousarray(np.asarray(Wout, dtype=np.float32).T),
        lam=np.ascontiguousarray(lam, dtype=np.float32),
        rmsw=np.ascontiguousarray(rmsw, dtype=np.float32),
        gw=np.ascontiguousarray(gw, dtype=np.float32),
        w1t=np.ascontiguousarray(np.asarray(w1, dtype=np.float32).transpose(0, 2, 1)),
        w3t=np.ascontiguousarray(np.asarray(w3, dtype=np.float32).transpose(0, 2, 1)),
        w2t=np.ascontiguousarray(np.asarray(w2, dtype=np.float32).transpose(0, 2, 1)),
        **consts,
    )
    x = np.ascontiguousarray(x, dtype=np.float32)
    return [dict(x=x[b], **shared) for b in range(x.shape[0])]


def kernel(L0_attn_x, L0_attn_Wq, L0_attn_Wkv, L0_attn_Wout, L0_attn_lambda_init,
           L0_attn_rms_weight, L0_ffn_gate_w, L0_ffn_w1, L0_ffn_w3, L0_ffn_w2):
    T = 2048
    nc = build_program(T=T)
    in_maps = make_inputs(L0_attn_x, L0_attn_Wq, L0_attn_Wkv, L0_attn_Wout,
                          L0_attn_lambda_init, L0_attn_rms_weight, L0_ffn_gate_w,
                          L0_ffn_w1, L0_ffn_w3, L0_ffn_w2, T)
    res = run_bass_kernel_spmd(nc, in_maps, core_ids=list(range(8)))
    out = np.stack([res.results[b]["out"] for b in range(8)], axis=0)
    return out.astype(np.float32)


# version 24
# speedup vs baseline: 1.0218x; 1.0218x over previous
import numpy as np

import concourse.bass as bass
import concourse.mybir as mybir
from concourse.bass import IndirectOffsetOnAxis
from concourse.bass_utils import run_bass_kernel_spmd

F32 = mybir.dt.float32
BF16 = mybir.dt.bfloat16
AF = mybir.ActivationFunctionType
OP = mybir.AluOpType
AX = mybir.AxisListType

D = 1024
HEADS = 16
KVH = 4
HD = 64
HALF = 32
NE = 8
HID = 2816
NHC = HID // 128
EPS_LN = 1e-5
EPS_RMS = 1e-6
SCALE = HALF ** -0.5


def cap_for(T):
    return max(128, ((5 * T // 16 + 127) // 128) * 128)


class Ctr:

    def __init__(self, sem, step=1):
        self.sem = sem
        self.step = step
        self.n = 0
        self._waited = {}

    def tick(self, instr, k=None):
        k = self.step if k is None else k
        instr.then_inc(self.sem, k)
        self.n += k
        return self.n

    def after(self, eng, at=None):
        v = self.n if at is None else at
        if v <= 0:
            return
        key = id(getattr(eng, "_e", eng))
        if self._waited.get(key, 0) >= v:
            return
        self._waited[key] = v
        eng.wait_ge(self.sem, v)


def chunks(total, size):
    out, off = [], 0
    while off < total:
        out.append((off, min(size, total - off)))
        off += size
    return out


def build_program(T=2048, upto="full", dbg=(), ne=NE):

    dbg = dict(dbg)
    NT = T // 128
    THALF = T // 2
    NTH = NT // 2
    NQB = T // 256

    nc = bass.Bass("TRN2", target_bir_lowering=False, num_swdge_queues=2)

    x_d = nc.dram_tensor("x", [T, D], F32, kind="ExternalInput").ap()
    wqt_d = nc.dram_tensor("wqt", [D, D], F32, kind="ExternalInput").ap()
    wkvt_d = nc.dram_tensor("wkvt", [D, 512], F32, kind="ExternalInput").ap()
    woutt_d = nc.dram_tensor("woutt", [D, D], F32, kind="ExternalInput").ap()
    lam_d = nc.dram_tensor("lam", [HEADS], F32, kind="ExternalInput").ap()
    rmsw_d = nc.dram_tensor("rmsw", [HD], F32, kind="ExternalInput").ap()
    gw_d = nc.dram_tensor("gw", [NE, D], F32, kind="ExternalInput").ap()
    w1t_d = nc.dram_tensor("w1t", [NE, D, HID], F32, kind="ExternalInput").ap()
    w3t_d = nc.dram_tensor("w3t", [NE, D, HID], F32, kind="ExternalInput").ap()
    w2t_d = nc.dram_tensor("w2t", [NE, HID, D], F32, kind="ExternalInput").ap()
    ident_d = nc.dram_tensor("ident", [128, 128], F32, kind="ExternalInput").ap()
    cos_d = nc.dram_tensor("cosc", [T, 16], F32, kind="ExternalInput").ap()
    sin_d = nc.dram_tensor("sinc", [T, 16], F32, kind="ExternalInput").ap()
    mask_d = nc.dram_tensor("maskc", [128, 512], F32, kind="ExternalInput").ap()
    ones_d = nc.dram_tensor("onesc", [1, 128], F32, kind="ExternalInput").ap()
    ltri_d = nc.dram_tensor("ltri", [128, 128], F32, kind="ExternalInput").ap()
    onesq_d = nc.dram_tensor("onesq", [128, 128], F32, kind="ExternalInput").ap()
    tokid_d = nc.dram_tensor("tokid", [128, T // 128], mybir.dt.int32, kind="ExternalInput").ap()
    ecap_d = nc.dram_tensor("ecap", [128, 8], F32, kind="ExternalInput").ap()
    pfx_d = nc.dram_tensor("pfx", [128, 128], F32, kind="ExternalInput").ap()
    CAP = cap_for(T)
    NCT = CAP // 128
    NSLOT = NE * CAP
    _xnc_h = nc.dram_tensor("xnc", [T + 1, 1040], BF16, kind="Internal")
    xnc_d = _xnc_h.ap()
    xnc_f = _xnc_h.bitcast(F32).ap()
    idx_d = nc.dram_tensor("idxs", [NSLOT + 128], mybir.dt.int32, kind="Internal").ap()
    g_d = nc.dram_tensor("gaccd", [T + 1, D], F32, kind="Internal").ap()
    out_d = nc.dram_tensor("out", [T, D], F32, kind="ExternalOutput").ap()
    dbg_d = {}
    for name, shape in dbg.items():
        dbg_d[name] = nc.dram_tensor("dbg_" + name, list(shape), F32, kind="ExternalOutput").ap()

    _CAP = cap_for(T)
    moe_arm = (HID // 128) * _CAP + 8192 + (HID // 128) * 1024 \
        + (_CAP // 128) * 1040 + 2 * 8 * _CAP + 8192
    AR = max(20 * T + (T // 128) * 260 + 20480, moe_arm) + 4096
    if 12 * T < 8192:
        AR += 8192
    from contextlib import ExitStack
    with ExitStack() as _stk:
        def _sb(name, shape, dt):
            return _stk.enter_context(nc.sbuf_tensor(name, shape, dt))

        xnT = _sb("xnT", [128, 8 * T], BF16)
        arena = _sb("arena", [128, AR], BF16)
        expb = _sb("expb", [128, 3072], BF16)
        scr = _sb("scr", [128, 1024], F32)
        stat = _sb("stat", [128, 64], F32)
        gsc = _sb("gsc", [128, 32], F32)
        cbig = _sb("cbig", [128, NT * 8], F32)
        identb = _sb("identb", [128, 128], BF16)
        cosb = _sb("cosb", [128, NT * 16], F32)
        sinb = _sb("sinb", [128, NT * 16], F32)
        maskb = _sb("maskb", [128, 512], BF16)
        zrow = _sb("zrow", [1, 512], BF16)
        lamb = _sb("lamb", [128, 16], F32)
        lam0 = _sb("lam0", [128, 16], F32)
        ltri_s = _sb("ltris", [128, 128], F32)
        onesq_s = _sb("onesqs", [128, 128], F32)
        tokid_sb = _sb("tokid_sb", [128, NT], mybir.dt.int32)
        ecap_sb = _sb("ecap_sb", [128, 8], F32)
        pfx_s = _sb("pfx_s", [128, 128], F32)
        msk = _sb("msk", [128, NT * 8], F32)
        rall = _sb("rall", [128, NT * 8], F32)
        sidx = _sb("sidx", [128, NT * 2], mybir.dt.int32)
        ibig = _sb("ibig", [128, NSLOT // 128], mybir.dt.int32)
        idx_sb = _sb("idx_sb", [128, NSLOT // 128], mybir.dt.int32)
        idxstg = _sb("idxstg", [NSLOT // 128, 128], mybir.dt.int32)
        idxstf = _sb("idxstf", [NSLOT // 128, 128], F32)
        idxf2 = _sb("idxf2", [128, NSLOT // 128], F32)
        cg_sb = _sb("cg_sb", [128, 2 * NCT], F32)
        rwb = _sb("rwb", [128, 64], F32)
        identf = _sb("identf", [128, 128], F32)
        gateTf = _sb("gateTf", [128, 64], F32)
        epsc = _sb("epsc", [128, 2], F32)
        def _sem(name):
            return _stk.enter_context(nc.semaphore(name))

        CQ = Ctr(_sem("smCQ"), 16)
        CG = Ctr(_sem("smCG"), 16)
        X = [Ctr(_sem("smX0"), 16), Ctr(_sem("smX1"), 16)]
        W13 = [Ctr(_sem("smWa"), 16), Ctr(_sem("smWb"), 16),
               Ctr(_sem("smWc"), 16), Ctr(_sem("smWd"), 16)]
        STG = Ctr(_sem("smSTG"), 16)
        SQT = Ctr(_sem("smSQT"), 16)
        SOT = Ctr(_sem("smSOT"), 16)
        OUT = Ctr(_sem("smOUT"), 16)
        DBG = Ctr(_sem("smDBG"), 16)
        sem_t = _sem("smT"); sem_v = _sem("smV"); sem_s = _sem("smS")
        TE = Ctr(sem_t)
        V = Ctr(sem_v)
        S = Ctr(sem_s)

        class _Gated:

            def __init__(self, eng, ctr):
                self._e = eng
                self._c = ctr

            def __getattr__(self, name):
                f = getattr(self._e, name)
                if name in ("wait_ge",):
                    return f

                def g(*a, **k):
                    self._c.after(self._e)
                    return f(*a, **k)
                return g

        te, sy, gp = nc.tensor, nc.sync, nc.gpsimd
        ve = _Gated(nc.vector, V)
        se = _Gated(nc.scalar, S)

        CG.tick(gp.dma_start(identb[:], ident_d))
        CG.tick(gp.dma_start(maskb[:], mask_d))
        CQ.tick(sy.dma_start(cosb[:].rearrange("p (n f) -> p n f", f=16),
                             cos_d.rearrange("(n p) f -> p n f", p=128)))
        CQ.tick(sy.dma_start(sinb[:].rearrange("p (n f) -> p n f", f=16),
                             sin_d.rearrange("(n p) f -> p n f", p=128)))
        CQ.tick(sy.dma_start(identf[:], ident_d))
        CQ.tick(sy.dma_start(scr[0:8, 0:1024], gw_d[:, :]))
        CQ.tick(sy.dma_start(ltri_s[:], ltri_d))
        CQ.tick(sy.dma_start(onesq_s[:], onesq_d))
        CQ.tick(sy.dma_start(tokid_sb[:], tokid_d))
        CQ.tick(sy.dma_start(ecap_sb[:], ecap_d))
        CQ.tick(sy.dma_start(pfx_s[:], pfx_d))
        V.tick(ve.memset(zrow[:], 0.0))
        V.tick(ve.memset(epsc[:, 0:1], EPS_LN))
        V.tick(ve.memset(epsc[:, 1:2], EPS_RMS))

        CQ.tick(sy.dma_start(lam0[:], lam_d.rearrange("(a b) -> a b", a=1).broadcast_to([128, 16])))
        CQ.tick(sy.dma_start(rwb[:], rmsw_d.rearrange("(a b) -> a b", a=1).broadcast_to([128, 64])))
        CQ.after(se)
        S.tick(se.activation(lamb[:], lam0[:], AF.Sigmoid))

        with nc.psum_tensor("pg", [128, 64], F32) as pg:
            CQ.after(te)
            S.after(te)
            V.after(te)
            for kc in range(8):
                TE.tick(te.transpose(pg[0 : 128, kc * 8 : kc * 8 + 8], scr[0:8, kc * 128 : (kc + 1) * 128], identf[0:8, 0:8]))
            TE.after(ve)
            V.tick(ve.tensor_copy(gateTf[:], pg[:]))
        CQ.after(ve)
        V.tick(ve.memset(scr[:], 0.0))
        V.tick(ve.memset(ibig[:], T))
        gate_t_ready = V.n
        GZ = Ctr(_sem("smGZ"), 16)
        V.after(sy)
        for tt in range(NT):
            GZ.tick(sy.dma_start(g_d[tt * 128 : (tt + 1) * 128, :], scr[:]))
        GZ.tick(sy.dma_start(g_d[T : T + 1, :], scr[0:1, :]))
        GZ.tick(sy.dma_start(xnc_d[T : T + 1, :], scr.bitcast(BF16)[0:1, 0:1040]))

        CG.after(te)
        CG.after(ve)
        _xt_off = NHC * CAP + ((NHC * CAP) & 1)
        xt = arena[:, _xt_off : _xt_off + 4096].bitcast(F32)
        scr2 = arena[:, _xt_off + 4096 : _xt_off + 6144].bitcast(F32)
        outf = arena[:, _xt_off + 6144 : _xt_off + 8192].bitcast(F32)
        _sb2 = NHC * CAP + 8192 + ((NHC * CAP) & 1)
        scr2b = arena[:, _sb2 : _sb2 + 2048].bitcast(F32)
        scrv = [scr2, scr2b]
        XND = [Ctr(_sem("smXN0"), 16), Ctr(_sem("smXN1"), 16)]
        CD = Ctr(_sem("smCD"), 16)
        GZ.after(ve)
        with nc.psum_tensor("pt0", [128, 1024], BF16) as pt0, \
             nc.psum_tensor("pt1", [128, 1024], BF16) as pt1, \
             nc.psum_tensor("ptf", [128, 1024], F32) as ptf, \
             nc.psum_tensor("pga", [128, 8], F32) as pga:
            xload = []
            for tt in range(min(2, NT)):
                X[tt % 2].tick(sy.dma_start(xt[:, (tt % 2) * 1024 : (tt % 2 + 1) * 1024],
                                            x_d[tt * 128 : (tt + 1) * 128, :]))
                xload.append(X[tt % 2].n)
            evac_done = [0, 0]
            gate_read = 0
            xnf_ev = 0
            sub_done = {}
            st1_done = {}
            bf16t_done = {}
            f32t_done = {}

            def stage1(tt):
                b = tt % 2
                sb = 16 * b
                xs = xt[:, b * 1024 : (b + 1) * 1024]
                X[b].after(ve, xload[tt])
                V.tick(ve.reduce_sum(stat[:, sb : sb + 1], xs, axis=AX.X))
                V.tick(ve.scalar_tensor_tensor(
                    scr[:], stat[:, sb : sb + 1].broadcast_to([128, 1024]),
                    -1.0 / D, xs, OP.mult, OP.add))
                sub_done[tt] = V.n
                if tt - 2 in f32t_done:
                    TE.after(ve, f32t_done[tt - 2])
                V.tick(ve.scalar_tensor_tensor(scrv[b][:], scr[:], 1.0, scr[:],
                                               OP.mult, OP.mult,
                                               accum_out=stat[:, sb + 2 : sb + 3]))
                V.after(se)
                S.tick(se.activation(stat[:, sb + 3 : sb + 4], stat[:, sb + 2 : sb + 3],
                                     AF.Sqrt, bias=epsc[:, 0:1], scale=1.0 / D))
                S.after(ve)
                V.tick(ve.reciprocal(stat[:, sb + 4 : sb + 5], stat[:, sb + 3 : sb + 4]))
                xn_tok = expb[:, b * 1024 : b * 1024 + 1024]
                XND[b].after(ve)
                if tt - 2 in bf16t_done:
                    TE.after(ve, bf16t_done[tt - 2])
                V.tick(ve.tensor_scalar(xn_tok, scr[:], stat[:, sb + 4 : sb + 5], None, OP.mult))
                V.tick(ve.tensor_scalar(scrv[b][:], scr[:], stat[:, sb + 4 : sb + 5], None, OP.mult))
                st1_done[tt] = V.n

            def stage2(tt):
                nonlocal gate_read, xnf_ev
                b = tt % 2
                pt = pt0 if b == 0 else pt1
                xn_tok = expb[:, b * 1024 : b * 1024 + 1024]
                V.after(te, st1_done[tt])
                V.after(te, evac_done[b])
                for kc in range(8):
                    TE.tick(te.transpose(pt[:, kc * 128 : (kc + 1) * 128],
                                         xn_tok[:, kc * 128 : (kc + 1) * 128], identb[:]))
                bf16t_done[tt] = TE.n
                TE.after(ve)
                V.tick(ve.tensor_copy(
                    xnT[:, 0 : 8 * T].rearrange("p (k t) -> p k t", t=T)[:, :, tt * 128 : (tt + 1) * 128],
                    pt[:].rearrange("p (k t) -> p k t", t=128)))
                evac_done[b] = V.n
                V.after(sy, st1_done[tt])
                XND[b].tick(sy.dma_start(xnc_d[tt * 128 : (tt + 1) * 128, 0:1024], xn_tok))
                if tt + 2 < NT:
                    V.after(sy, sub_done[tt])
                    X[b].tick(sy.dma_start(xt[:, b * 1024 : (b + 1) * 1024],
                                           x_d[(tt + 2) * 128 : (tt + 3) * 128, :]))
                    xload.append(X[b].n)
                V.after(te, xnf_ev)
                for kc in range(8):
                    TE.tick(te.transpose(ptf[:, kc * 128 : (kc + 1) * 128],
                                         scrv[b][:, kc * 128 : (kc + 1) * 128], identf[:]))
                f32t_done[tt] = TE.n
                TE.after(se)
                S.tick(se.copy(outf[:], ptf[:]))
                xnf_ev_s = S.n
                S.after(te, xnf_ev_s)
                V.after(te, gate_read)
                for kc in range(8):
                    TE.tick(te.matmul(pga[:],
                                      outf[:, kc * 128 : (kc + 1) * 128],
                                      gateTf[:, kc * 8 : (kc + 1) * 8],
                                      start=(kc == 0), stop=(kc == 7)))
                TE.after(ve)
                V.tick(ve.tensor_copy(cbig[:, tt * 8 : (tt + 1) * 8], pga[:]))
                gate_read = V.n
                xnf_ev = 0
                return xnf_ev_s

            s_ev = 0
            for tt in range(NT):
                stage1(tt)
                if tt >= 1:
                    if s_ev:
                        S.after(te, s_ev)
                    s_ev = stage2(tt - 1)
            if s_ev:
                S.after(te, s_ev)
            stage2(NT - 1)

        NT8 = NT * 8
        c3 = cbig[:, 0:NT8].rearrange("p (t e) -> p t e", e=8)
        V.tick(ve.reduce_max(stat[:, 32 : 32 + NT], c3, axis=AX.X))
        V.tick(ve.tensor_tensor(scr[:, 0:NT8].rearrange("p (t e) -> p t e", e=8), c3,
                                stat[:, 32 : 32 + NT].unsqueeze(2).broadcast_to([128, NT, 8]),
                                OP.subtract))
        V.after(se)
        S.tick(se.activation(scr[:, 256 : 256 + NT8], scr[:, 0:NT8], AF.Exp))
        z = scr[:, 256 : 256 + NT8]
        z3 = z.rearrange("p (t e) -> p t e", e=8)
        m1 = scr[:, 512 : 512 + NT8]
        S.after(ve)
        V.tick(ve.reduce_max(stat[:, 32 : 32 + NT], z3, axis=AX.X))
        V.tick(ve.tensor_tensor(m1.rearrange("p (t e) -> p t e", e=8), z3,
                                stat[:, 32 : 32 + NT].unsqueeze(2).broadcast_to([128, NT, 8]),
                                OP.is_ge))
        V.tick(ve.scalar_tensor_tensor(scr[:, 0:NT8], m1, -1e30, z, OP.mult, OP.add))
        V.tick(ve.reduce_max(stat[:, 48 : 48 + NT],
                             scr[:, 0:NT8].rearrange("p (t e) -> p t e", e=8), axis=AX.X))
        V.tick(ve.tensor_tensor(msk[:, 0:NT8].rearrange("p (t e) -> p t e", e=8), z3,
                                stat[:, 48 : 48 + NT].unsqueeze(2).broadcast_to([128, NT, 8]),
                                OP.is_ge))
        V.tick(ve.tensor_mul(scr[:, 0:NT8], msk[:, 0:NT8], z))
        zk3 = scr[:, 0:NT8].rearrange("p (t e) -> p t e", e=8)
        V.tick(ve.reduce_sum(stat[:, 32 : 32 + NT], zk3, axis=AX.X))
        V.tick(ve.reciprocal(stat[:, 48 : 48 + NT], stat[:, 32 : 32 + NT]))
        V.tick(ve.tensor_tensor(c3, zk3,
                                stat[:, 48 : 48 + NT].unsqueeze(2).broadcast_to([128, NT, 8]),
                                OP.mult))
        coeff_done = V.n
        V.after(sy, coeff_done)
        for tt in range(NT):
            CD.tick(sy.dma_start(xnc_f[tt * 128 : (tt + 1) * 128, 512:520],
                                 cbig[:, tt * 8 : (tt + 1) * 8]))

        if "xn" in dbg:
            V.after(gp)
            DBG.tick(gp.dma_start(dbg_d["xn"], xnT[:, 0 : 8 * T]))
        if "gate" in dbg:
            V.after(gp)
            DBG.tick(gp.dma_start(dbg_d["gate"], cbig[:, 0 : NT * 8]))

        if upto == "ln":
            V.after(sy)
            OUT.tick(sy.dma_start(out_d[0:128, :], scr[:]))
            OUT.after(sy)
            DBG.after(sy)
            XND[0].after(sy)
            XND[1].after(sy)
            CD.after(sy)
            GZ.after(sy)
            return nc

        IDXI = Ctr(_sem("smIDXI"), 16)
        SCI = Ctr(_sem("smSCI"), 16)
        IDXL = Ctr(_sem("smIDXL"), 16)
        NSLOTF = float(NSLOT)
        with nc.psum_tensor("prk", [128, 128], F32) as prk, \
             nc.psum_tensor("ptrk", [128, 128], F32) as ptr, \
             nc.psum_tensor("pbr", [128, 128], F32) as pbr:
            CQ.after(te)
            mask_v = V.n
            V.after(te, mask_v)
            TE.tick(te.matmul(prk[:, 0:NT8], ltri_s[:], msk[:, 0:NT8],
                              start=True, stop=True))
            TE.after(ve)
            wsb = scr2
            V.tick(ve.tensor_copy(wsb[:, 0:NT8], prk[:, 0:NT8]))
            winv = V.n
            V.after(te, winv)
            TE.tick(te.transpose(ptr[0:NT8, 0:128], wsb[:, 0:NT8], identf[:]))
            TE.after(ve)
            V.tick(ve.memset(gsc[:, 0:1], 0.0))
            V.tick(ve.tensor_copy(gsc[0:NT8, 0:1], ptr[0:NT8, 127:128]))
            cnt_v = V.n
            V.after(te, cnt_v)
            TE.tick(te.matmul(pbr[:, 0:1], pfx_s[:], gsc[:, 0:1], start=True, stop=True))
            TE.after(ve)
            V.tick(ve.tensor_copy(gsc[:, 1:2], pbr[:, 0:1]))
            off_v = V.n
            V.after(te, off_v)
            TE.tick(te.transpose(ptr[0:1, 0:128], gsc[:, 1:2], identf[:]))
            TE.after(ve)
            V.tick(ve.tensor_copy(scr[0:1, 768:896], ptr[0:1, 0:128]))
            offr_v = V.n
            V.after(te, offr_v)
            TE.tick(te.matmul(pbr[:, 0:128], onesq_s[0:1, 0:128], scr[0:1, 768:896],
                              start=True, stop=True))
            TE.after(ve)
            V.tick(ve.tensor_tensor(rall[:, 0:NT8], wsb[:, 0:NT8], pbr[:, 0:NT8], OP.add))
            m2 = scr[:, 0:NT8]
            V.tick(ve.tensor_sub(m2, msk[:, 0:NT8], m1))
            radj = scr[:, 256 : 256 + NT8]
            V.tick(ve.tensor_tensor(radj.rearrange("p (t e) -> p t e", e=8),
                                    rall[:, 0:NT8].rearrange("p (t e) -> p t e", e=8),
                                    ecap_sb[:].unsqueeze(1).broadcast_to([128, NT, 8]),
                                    OP.add))
            sidx3 = sidx[:, 0 : NT * 2].rearrange("p (t j) -> p t j", j=2)
            for j, mj in ((0, m1), (1, m2)):
                V.tick(ve.tensor_mul(scr[:, 128 : 128 + NT8], mj, radj))
                V.tick(ve.reduce_sum(stat[:, 32 : 32 + NT],
                                     scr[:, 128 : 128 + NT8].rearrange("p (t e) -> p t e", e=8),
                                     axis=AX.X))
                V.tick(ve.tensor_mul(scr[:, 128 : 128 + NT8], mj, rall[:, 0:NT8]))
                V.tick(ve.reduce_sum(stat[:, 48 : 48 + NT],
                                     scr[:, 128 : 128 + NT8].rearrange("p (t e) -> p t e", e=8),
                                     axis=AX.X))
                V.tick(ve.tensor_scalar(stat[:, 0:NT], stat[:, 48 : 48 + NT],
                                        float(CAP), None, OP.is_gt))
                V.tick(ve.scalar_tensor_tensor(stat[:, 48 : 48 + NT], stat[:, 0:NT],
                                               NSLOTF, stat[:, 32 : 32 + NT],
                                               OP.mult, OP.add))
                V.tick(ve.tensor_scalar(sidx3[:, :, j : j + 1],
                                        stat[:, 48 : 48 + NT].unsqueeze(2),
                                        NSLOTF, None, OP.min))
            sidx_v = V.n
        V.after(gp, sidx_v)
        IDXI.tick(gp.dma_start(idx_d[0:NSLOT].rearrange("(p a) -> p a", a=NSLOT // 128), ibig[:]))
        IDXI.after(gp)
        for tt in range(NT):
            for j in range(2):
                SCI.tick(gp.indirect_dma_start(
                    idx_d.rearrange("(a b) -> a b", b=1),
                    IndirectOffsetOnAxis(sidx[:, tt * 2 + j : tt * 2 + j + 1], 0),
                    tokid_sb[:, tt : tt + 1], None,
                    oob_is_err=False))
        SCI.after(gp)
        IDXL.tick(gp.dma_start(idxstg[:], idx_d[0:NSLOT].rearrange("(a b) -> a b", b=128)))
        NA = NSLOT // 128
        with nc.psum_tensor("ptx", [128, 64], F32) as ptx:
            IDXL.after(ve)
            V.tick(ve.tensor_copy(idxstf[:], idxstg[:]))
            V.after(te)
            TE.tick(te.transpose(ptx[:, 0:NA], idxstf[:], identf[0:NA, 0:NA]))
            TE.after(ve)
            V.tick(ve.tensor_copy(idxf2[:], ptx[:, 0:NA]))
            V.after(ve)
            V.tick(ve.tensor_copy(idx_sb[:], idxf2[:]))
        idx_ready = V.n

        HH0 = 0
        WST = NHC * CAP
        w1t_o = [WST, WST + 1024, WST + 2048, WST + 3072]
        w3t_o = [WST + 4096, WST + 5120, WST + 6144, WST + 7168]
        W2T = WST + 8192
        GX0 = W2T + NHC * 1024
        GXT0 = GX0 + NCT * 1040
        SC0 = GXT0 + 2 * 8 * CAP
        assert SC0 + 8192 <= AR
        assert GX0 % 2 == 0 and SC0 % 4 == 0

        n_tq = chunks(CAP, 512)
        with nc.psum_tensor("mt0", [128, 1024], BF16) as mt0, \
             nc.psum_tensor("mt1", [128, 1024], BF16) as mt1, \
             nc.psum_tensor("h1a", [128, 512], F32) as h1a, \
             nc.psum_tensor("h3a", [128, 512], F32) as h3a, \
             nc.psum_tensor("h1b", [128, 512], F32) as h1b, \
             nc.psum_tensor("h3b", [128, 512], F32) as h3b:
            wload = {}
            stage_free = {}
            GX = Ctr(_sem("smGX"), 16)
            SCAT = [Ctr(_sem("smSCA"), 16), Ctr(_sem("smSCB"), 16)]
            scat_done_prev = [0, 0]
            W2C = Ctr(_sem("smW2C"), 16)
            w2ld = {}
            _w2ch = [(0, 3), (3, 6), (6, 9), (9, 12), (12, 15), (15, 18), (18, 20), (20, 22)]

            def issue_w13(e, hc):
                if e >= ne:
                    return
                slot = (e * NHC + hc) % 4
                key = ("s13", slot)
                if key in stage_free:
                    TE.after(gp, stage_free[key])
                W13[slot].tick(gp.dma_start(
                    arena[:, w1t_o[slot] : w1t_o[slot] + 1024].rearrange("p (k h) -> p k h", h=128),
                    w1t_d[e].rearrange("(k p) h -> p k h", p=128)[:, :, hc * 128 : (hc + 1) * 128]))
                W13[slot].tick(gp.dma_start(
                    arena[:, w3t_o[slot] : w3t_o[slot] + 1024].rearrange("p (k h) -> p k h", h=128),
                    w3t_d[e].rearrange("(k p) h -> p k h", p=128)[:, :, hc * 128 : (hc + 1) * 128]))
                wload[("w13", e, hc)] = (W13[slot], W13[slot].n)

            def issue_w2(e, j):
                if e >= ne:
                    return
                hc0, hc1 = _w2ch[j]
                TE.after(gp, w2t_free)
                W2C.tick(gp.dma_start(
                    arena[:, W2T + hc0 * 1024 : W2T + hc1 * 1024].rearrange("p (h c) -> p h c", c=1024),
                    w2t_d[e].rearrange("(h p) c -> p h c", p=128)[:, hc0:hc1, :]))
                w2ld[(e, j)] = W2C.n

            WKVT = AR - 4096
            for _kc in range(8):
                STG.tick(gp.dma_start(arena[:, WKVT + _kc * 512 : WKVT + (_kc + 1) * 512],
                                      wkvt_d[_kc * 128 : (_kc + 1) * 128, :]))
            stg_kv = STG.n
            for _hc0 in range(4):
                issue_w13(0, _hc0)
            V.after(te)
            S.after(te)

            mtev = [0, 0]
            hfree = [0, 0]
            hsil = [0, 0]
            w2t_free = 0
            gx_te_free = 0
            gx_ve_free = 0
            gxt_free = [0, 0]
            gxt_vready = [0, 0]
            last_hh = [0]

            stages = []
            for e in range(ne):
                stages.append(("G", e, 0))
                for hc in range(NHC):
                    stages.append(("A", e, hc))
                stages.append(("B", e, 0))

            def prep(st):
                nonlocal gx_te_free, gx_ve_free
                kind, e, x = st
                if kind == "G":
                    gslot = e % 2
                    gxt = GXT0 + gslot * 8 * CAP
                    TE.after(gp, gx_te_free)
                    V.after(gp, max(gx_ve_free, idx_ready))
                    GZ.after(gp)
                    for tt2 in range(NT):
                        XND[tt2 % 2].after(gp)
                    CD.after(gp)
                    for stt in range(NCT):
                        GX.tick(gp.indirect_dma_start(
                            arena[:, GX0 + stt * 1040 : GX0 + (stt + 1) * 1040], None,
                            xnc_d, IndirectOffsetOnAxis(idx_sb[:, e * NCT + stt : e * NCT + stt + 1], 0),
                            oob_is_err=False))
                    GX.after(te)
                    for stt in range(NCT):
                        slot = stt % 2
                        mtb = mt0 if slot == 0 else mt1
                        V.after(te, mtev[slot])
                        for kc in range(8):
                            TE.tick(te.transpose(mtb[:, kc * 128 : (kc + 1) * 128],
                                                 arena[:, GX0 + stt * 1040 + kc * 128 : GX0 + stt * 1040 + (kc + 1) * 128],
                                                 identb[:]))
                        TE.after(ve)
                        if gxt_free[gslot]:
                            TE.after(ve, gxt_free[gslot])
                        V.tick(ve.tensor_copy(
                            arena[:, gxt : gxt + 8 * CAP].rearrange("p (k t) -> p k t", t=CAP)
                                 [:, :, stt * 128 : (stt + 1) * 128],
                            mtb[:].rearrange("p (k t) -> p k t", t=128)))
                        mtev[slot] = V.n
                    gx_te_free = TE.n
                    gxt_vready[gslot] = V.n
                    gxf = arena[:, GX0 : GX0 + NCT * 1040].bitcast(F32)
                    for stt in range(NCT):
                        V.after(ve)
                        V.tick(ve.tensor_copy(cg_sb[:, gslot * NCT + stt : gslot * NCT + stt + 1],
                                              gxf[:, stt * 520 + 512 + e : stt * 520 + 512 + e + 1]))
                    gx_ve_free = V.n
                    return {"gslot": gslot, "gxt": gxt, "cg_v": V.n}
                elif kind == "A":
                    hc = x
                    return {"slot": (e * NHC + hc) % 4, "wt": wload[("w13", e, hc)]}
                else:
                    return {"w2t_ready": w2ld[(e, len(_w2ch) - 1)]}

            def compute(st, pr):
                nonlocal w2t_free
                kind, e, x = st
                if kind == "G":
                    return
                if kind == "A":
                    hc = x
                    slot = pr["slot"]
                    _wc, _wv = pr["wt"]
                    _wc.after(te, _wv)
                    V.after(te, gxt_vready[e % 2])
                    gxt = GXT0 + (e % 2) * 8 * CAP
                    for qi, (toff, tlen) in enumerate(n_tq):
                        hb = qi % 2
                        h1 = h1a if hb == 0 else h1b
                        h3 = h3a if hb == 0 else h3b
                        V.after(te, hfree[hb])
                        S.after(te, hsil[hb])
                        for kc in range(8):
                            TE.tick(te.matmul(h1[:, 0:tlen],
                                              arena[:, w1t_o[slot] + kc * 128 : w1t_o[slot] + (kc + 1) * 128],
                                              arena[:, gxt + kc * CAP + toff : gxt + kc * CAP + toff + tlen],
                                              start=(kc == 0), stop=(kc == 7)))
                        h1_done = TE.n
                        for kc in range(8):
                            TE.tick(te.matmul(h3[:, 0:tlen],
                                              arena[:, w3t_o[slot] + kc * 128 : w3t_o[slot] + (kc + 1) * 128],
                                              arena[:, gxt + kc * CAP + toff : gxt + kc * CAP + toff + tlen],
                                              start=(kc == 0), stop=(kc == 7)))
                        h3_done = TE.n
                        if hc == NHC - 1:
                            gxt_free[e % 2] = TE.n
                        if qi == len(n_tq) - 1:
                            stage_free[("s13", slot)] = TE.n
                            if hc + 4 < NHC:
                                issue_w13(e, hc + 4)
                            else:
                                issue_w13(e + 1, hc + 4 - NHC)
                            if 3 <= hc <= 10:
                                issue_w2(e, hc - 3)
                        TE.after(se, h1_done)
                        sil = expb[:, 1024 + hb * 512 : 1024 + hb * 512 + 512]
                        S.tick(se.activation(sil[:, 0:tlen], h1[:, 0:tlen], AF.Silu))
                        hsil[hb] = S.n
                        S.after(ve, hsil[hb])
                        TE.after(ve, h3_done)
                        V.tick(ve.tensor_tensor(
                            arena[:, HH0 + hc * CAP + toff : HH0 + hc * CAP + toff + tlen],
                            sil[:, 0:tlen], h3[:, 0:tlen], OP.mult))
                        hfree[hb] = V.n
                        last_hh[0] = V.n
                else:
                    gslot = e % 2
                    scf = arena[:, SC0 : SC0 + 8192].bitcast(F32)
                    for ti in range(NCT):
                        ssl = ti % 2
                        sc_v = scf[:, ssl * 2048 : ssl * 2048 + 1024]
                        SCAT[ssl].after(ve)
                        for ch in range(2):
                            ob = ch
                            opsum = h1a if ob == 0 else h1b
                            V.after(te, hfree[ob])
                            S.after(te, hsil[ob])
                            W2C.after(te, pr["w2t_ready"])
                            V.after(te, last_hh[0])
                            for hc in range(NHC):
                                TE.tick(te.matmul(opsum[:, 0:512],
                                                  arena[:, HH0 + hc * CAP + ti * 128 : HH0 + hc * CAP + (ti + 1) * 128],
                                                  arena[:, W2T + hc * 1024 + ch * 512 : W2T + hc * 1024 + ch * 512 + 512],
                                                  start=(hc == 0), stop=(hc == NHC - 1)))
                            o_done = TE.n
                            TE.after(ve, o_done)
                            V.tick(ve.tensor_scalar(sc_v[:, ch * 512 : ch * 512 + 512], opsum[:, 0:512],
                                                    cg_sb[:, gslot * NCT + ti : gslot * NCT + ti + 1],
                                                    None, OP.mult))
                            hfree[ob] = V.n
                        sc_ready = V.n
                        V.after(gp, sc_ready)
                        GZ.after(gp)
                        SCAT[0].after(gp)
                        SCAT[1].after(gp)
                        SCAT[ssl].tick(gp.indirect_dma_start(
                            g_d, IndirectOffsetOnAxis(idx_sb[:, e * NCT + ti : e * NCT + ti + 1], 0),
                            sc_v, None,
                            oob_is_err=False, compute_op=OP.add))
                    w2t_free = TE.n
                    scat_done_prev[0] = SCAT[0].n
                    scat_done_prev[1] = SCAT[1].n

            if stages:
                pr_prev = prep(stages[0])
                for i in range(1, len(stages)):
                    pr_next = prep(stages[i])
                    compute(stages[i - 1], pr_prev)
                    pr_prev = pr_next
                compute(stages[-1], pr_prev)
            SCAT[0].after(sy)
            SCAT[1].after(sy)

        if "gmoe" in dbg:
            SCAT[0].after(sy)
            SCAT[1].after(sy)
            DBG.tick(sy.dma_start(dbg_d["gmoe"], g_d[0:T, :]))

        if upto == "moe":
            V.after(sy)
            OUT.tick(sy.dma_start(out_d[0:128, :], scr[:]))
            OUT.after(sy)
            DBG.after(sy)
            return nc

        QT0 = 0
        NQT = 12
        KT0 = NQT * T
        VA0 = KT0 + 8 * T
        WQT = VA0 + NT * 260
        WOUTT = WQT + 12288
        assert WOUTT + 8192 <= AR
        ST0 = QT0 if 12 * T >= 8192 else AR - 8192

        _kpos = {0: 0, 3: 1, 6: 2, 1: 3, 4: 4, 7: 5, 2: 6, 5: 7}

        def kslot(g, s):
            sl = 2 * g + s
            return KT0 + _kpos[sl] * T, (sl % 3) * 32

        _qmap = {}
        _base_fill = [0, 0, 0]
        for _h in range(HEADS):
            for _s in range(2):
                _b = (2 * (_h // 4) + _s) % 3
                _qmap[(_h, _s)] = (_base_fill[_b], _b * 32)
                _base_fill[_b] += 1
        assert max(_base_fill) <= NQT

        def qslot(h, s):
            tile, base = _qmap[(h, s)]
            return QT0 + tile * T, base

        _qbase_pieces = {0: [], 1: [], 2: []}
        for _h in range(HEADS):
            for _s in range(2):
                _b = (2 * (_h // 4) + _s) % 3
                _qbase_pieces[_b].append((_h, _s))
        assert [len(_qbase_pieces[b]) for b in range(3)] == [12, 12, 8]

        TE.after(gp)
        V.after(gp)
        S.after(gp)
        for kc in range(8):
            SQT.tick(gp.dma_start(arena[:, WQT + kc * 1024 : WQT + (kc + 1) * 1024],
                                  wqt_d[kc * 128 : (kc + 1) * 128, :]))
        for kc in range(8):
            SOT.tick(gp.dma_start(arena[:, WOUTT + kc * 1024 : WOUTT + (kc + 1) * 1024],
                                  woutt_d[kc * 128 : (kc + 1) * 128, :]))

        with nc.psum_tensor("kvA", [128, 512], F32) as kvA, \
             nc.psum_tensor("q1A", [128, 512], F32) as q1A, \
             nc.psum_tensor("q2A", [128, 512], F32) as q2A, \
             nc.psum_tensor("kvB", [128, 512], F32) as kvB, \
             nc.psum_tensor("q1B", [128, 512], F32) as q1B, \
             nc.psum_tensor("q2B", [128, 512], F32) as q2B, \
             nc.psum_tensor("ptrV", [128, 1024], BF16) as ptrV, \
             nc.psum_tensor("ptrS", [128, 1024], BF16) as ptrS:
            psets = [(kvA, q1A, q2A), (kvB, q1B, q2B)]
            TE.after(ve)
            V.tick(ve.memset(arena[:, QT0 : KT0 + 8 * T], 0.0))
            zq_v = V.n
            vread_set = [0, 0]
            kv_done = {}
            q_done = {}
            pv_free = 0
            ps_free = 0
            s_memset_gate = [False]

            def stage_proj(tt):
                s = tt % 2
                kvp, q1p, q2p = psets[s]
                if tt == 0:
                    STG.after(te, stg_kv)
                V.after(te, vread_set[s])
                for kc in range(8):
                    TE.tick(te.matmul(kvp[:, 0:512],
                                      xnT[:, kc * T + tt * 128 : kc * T + tt * 128 + 128],
                                      arena[:, WKVT + kc * 512 : WKVT + (kc + 1) * 512],
                                      start=(kc == 0), stop=(kc == 7)))
                kv_done[tt] = TE.n
                if tt == 0:
                    SQT.after(te)
                for h2 in range(2):
                    pq = q1p if h2 == 0 else q2p
                    for kc in range(8):
                        TE.tick(te.matmul(pq[:, 0:512],
                                          xnT[:, kc * T + tt * 128 : kc * T + tt * 128 + 128],
                                          arena[:, WQT + kc * 1024 + h2 * 512 : WQT + kc * 1024 + h2 * 512 + 512],
                                          start=(kc == 0), stop=(kc == 7)))
                q_done[tt] = TE.n

            def stage_rope(tt):
                nonlocal pv_free, ps_free
                s = tt % 2
                kvp, q1p, q2p = psets[s]
                TE.after(ve, kv_done[tt])
                V.tick(ve.memset(arena[:, VA0 + tt * 260 : VA0 + (tt + 1) * 260], 1.0))
                V.tick(ve.tensor_copy(
                    arena[:, VA0 + tt * 260 : VA0 + (tt + 1) * 260].rearrange("p (g d) -> p g d", d=65)[:, :, 0:64],
                    kvp[:, 256:512].rearrange("p (g d) -> p g d", d=64)))
                kx1 = kvp[:, 0:256].rearrange("p (a j two) -> p a j two", two=2, j=16)[:, :, :, 0:1].squeeze(3)
                kx2 = kvp[:, 0:256].rearrange("p (a j two) -> p a j two", two=2, j=16)[:, :, :, 1:2].squeeze(3)
                cs8 = cosb[:, tt * 16 : (tt + 1) * 16].unsqueeze(1).broadcast_to([128, 8, 16])
                sn8 = sinb[:, tt * 16 : (tt + 1) * 16].unsqueeze(1).broadcast_to([128, 8, 16])
                V.tick(ve.tensor_tensor(scr[:, 0:128].rearrange("p (a j) -> p a j", j=16), kx1, cs8, OP.mult))
                V.tick(ve.tensor_tensor(scr[:, 128:256].rearrange("p (a j) -> p a j", j=16), kx2, sn8, OP.mult))
                V.tick(ve.tensor_tensor(scr[:, 256:384].rearrange("p (a j) -> p a j", j=16), kx1, sn8, OP.mult))
                V.tick(ve.tensor_tensor(scr[:, 384:512].rearrange("p (a j) -> p a j", j=16), kx2, cs8, OP.mult))
                kro = expb[:, 0:256]
                V.tick(ve.tensor_tensor(
                    kro.rearrange("p (a two) -> p a two", two=2)[:, :, 0:1].squeeze(2),
                    scr[:, 0:128], scr[:, 128:256], OP.subtract))
                V.tick(ve.tensor_tensor(
                    kro.rearrange("p (a two) -> p a two", two=2)[:, :, 1:2].squeeze(2),
                    scr[:, 256:384], scr[:, 384:512], OP.add))
                krope_done = V.n
                V.after(te, krope_done)
                V.after(te, pv_free)
                _kgroups = [(0, [0, 3, 6]), (32, [1, 4, 7]), (64, [2, 5])]
                _col = 0
                _kcols = {}
                for _base, _sls in _kgroups:
                    for _sl in _sls:
                        _g, _s2 = _sl // 2, _sl % 2
                        TE.tick(te.transpose(ptrV[_base : _base + 32, _col * 128 : _col * 128 + 128],
                                             kro[:, _g * 64 + _s2 * 32 : _g * 64 + _s2 * 32 + 32], identb[:]))
                        _col += 1
                ktr = TE.n
                TE.after(ve, ktr)
                _col = 0
                for _base, _sls in _kgroups:
                    _n = len(_sls)
                    _t0 = _kpos[_sls[0]]
                    V.tick(ve.tensor_copy(
                        arena[_base : _base + 32, KT0 + _t0 * T : KT0 + (_t0 + _n) * T]
                            .rearrange("p (a c) -> p a c", c=T)[:, :, tt * 128 : (tt + 1) * 128],
                        ptrV[_base : _base + 32, _col * 128 : (_col + _n) * 128]
                            .rearrange("p (a c) -> p a c", c=128)))
                    _col += _n
                k_pv, k_ps = V.n, S.n
                TE.after(ve, q_done[tt])
                qro = expb[:, 0:1024]
                c16 = cosb[:, tt * 16 : (tt + 1) * 16].unsqueeze(1).broadcast_to([128, 16, 16])
                s16 = sinb[:, tt * 16 : (tt + 1) * 16].unsqueeze(1).broadcast_to([128, 16, 16])
                for hb, pq in enumerate([q1p, q2p]):
                    qx1 = pq[:, 0:512].rearrange("p (a j two) -> p a j two", two=2, j=16)[:, :, :, 0:1].squeeze(3)
                    qx2 = pq[:, 0:512].rearrange("p (a j two) -> p a j two", two=2, j=16)[:, :, :, 1:2].squeeze(3)
                    V.tick(ve.tensor_tensor(scr[:, 0:256].rearrange("p (a j) -> p a j", j=16), qx1, c16, OP.mult))
                    V.tick(ve.tensor_tensor(scr[:, 256:512].rearrange("p (a j) -> p a j", j=16), qx2, s16, OP.mult))
                    V.tick(ve.tensor_tensor(scr[:, 512:768].rearrange("p (a j) -> p a j", j=16), qx1, s16, OP.mult))
                    V.tick(ve.tensor_tensor(scr[:, 768:1024].rearrange("p (a j) -> p a j", j=16), qx2, c16, OP.mult))
                    qd = qro[:, hb * 512 : (hb + 1) * 512].rearrange("p (a two) -> p a two", two=2)
                    V.tick(ve.tensor_tensor(qd[:, :, 0:1].squeeze(2), scr[:, 0:256], scr[:, 256:512], OP.subtract))
                    V.tick(ve.tensor_tensor(qd[:, :, 1:2].squeeze(2), scr[:, 512:768], scr[:, 768:1024], OP.add))
                vread_set[s] = V.n
                qrope_done = V.n
                V.after(te, qrope_done)
                V.after(te, k_pv)
                S.after(te, ps_free)
                if not s_memset_gate[0]:
                    V.after(se, zq_v)
                    s_memset_gate[0] = True
                def _qround(chunks):
                    cols = {id(ptrV): 0, id(ptrS): 0}
                    plan = []
                    for bank, base, t0, pieces in chunks:
                        c0 = cols[id(bank)]
                        for j, (h, s2) in enumerate(pieces):
                            TE.tick(te.transpose(
                                bank[base : base + 32, (c0 + j) * 128 : (c0 + j) * 128 + 128],
                                qro[:, h * 64 + s2 * 32 : h * 64 + s2 * 32 + 32], identb[:]))
                        cols[id(bank)] = c0 + len(pieces)
                        plan.append((bank, base, t0, c0, len(pieces)))
                    qtr = TE.n
                    TE.after(ve, qtr)
                    TE.after(se, qtr)
                    for bank, base, t0, c0, n in plan:
                        dst = arena[base : base + 32, QT0 + t0 * T : QT0 + (t0 + n) * T] \
                            .rearrange("p (a c) -> p a c", c=T)[:, :, tt * 128 : (tt + 1) * 128]
                        srcap = bank[base : base + 32, c0 * 128 : (c0 + n) * 128] \
                            .rearrange("p (a c) -> p a c", c=128)
                        if bank is ptrV:
                            V.tick(ve.tensor_copy(dst, srcap))
                        else:
                            S.tick(se.copy(dst, srcap))
                b0, b1, b2 = _qbase_pieces[0], _qbase_pieces[1], _qbase_pieces[2]
                _qround([(ptrV, 0, 0, b0[0:8]), (ptrS, 32, 0, b1[0:8])])
                V.after(te, V.n)
                S.after(te, S.n)
                _qround([(ptrV, 0, 8, b0[8:12]), (ptrV, 64, 0, b2[0:4]),
                         (ptrS, 32, 8, b1[8:12]), (ptrS, 64, 4, b2[4:8])])
                pv_free, ps_free = V.n, S.n

            for tt in range(NT):
                stage_proj(tt)
                if tt >= 1:
                    stage_rope(tt - 1)
            stage_rope(NT - 1)

        if upto == "proj":
            V.after(sy)
            OUT.tick(sy.dma_start(out_d[0:128, :], scr[:]))
            OUT.after(sy)
            DBG.after(sy)
            return nc

        y_all = xnT
        with nc.psum_tensor("sA", [128, 1024], F32) as sA, \
             nc.psum_tensor("sB", [128, 1024], F32) as sB, \
             nc.psum_tensor("sC", [128, 1024], F32) as sC, \
             nc.psum_tensor("uA", [128, 260], F32) as uA, \
             nc.psum_tensor("uB", [128, 260], F32) as uB:
            s_banks = [sA, sB, sC]
            u_sets = [(uA, uB)]
            V.after(te)
            S.after(te)
            exp_done = [0, 0, 0]
            mask_done = [0, 0, 0]
            av_done = [0, 0, 0]
            uev_set = [0]
            pend_cmb = [None]

            def emit_combine():
                if pend_cmb[0] is None:
                    return
                ch0, ch1, cqb, cua, cub, csi, cavn = pend_cmb[0]
                pend_cmb[0] = None
                TE.after(ve, cavn)
                V.tick(ve.tensor_copy(scr[:, 0:260], cua[:, 0:260]))
                V.tick(ve.tensor_copy(scr[:, 512:772], cub[:, 0:260]))
                uev_set[csi] = V.n
                V.tick(ve.reciprocal(stat[:, 16:20],
                                     scr[:, 0:260].rearrange("p (a d) -> p a d", d=65)[:, :, 64:65].squeeze(2)))
                V.tick(ve.reciprocal(stat[:, 20:24],
                                     scr[:, 512:772].rearrange("p (a d) -> p a d", d=65)[:, :, 64:65].squeeze(2)))
                for hi, h in enumerate([ch0, ch1]):
                    ub = 0 if hi == 0 else 512
                    rbase = 16 + hi * 4
                    for qt in range(2):
                        col1 = ub + (0 * 2 + qt) * 65
                        col2 = ub + (1 * 2 + qt) * 65
                        tt = cqb * 2 + qt
                        r1c = stat[:, rbase + qt : rbase + qt + 1]
                        r2c = stat[:, rbase + 2 + qt : rbase + 2 + qt + 1]
                        V.tick(ve.tensor_scalar(scr[:, 300:364], scr[:, col1 : col1 + 64], r1c, None, OP.mult))
                        V.tick(ve.tensor_scalar(scr[:, 364:428], scr[:, col2 : col2 + 64], r2c,
                                                lamb[:, h : h + 1], OP.mult, OP.mult))
                        V.tick(ve.tensor_tensor(y_all[:, tt * 1024 + h * 64 : tt * 1024 + h * 64 + 64],
                                                scr[:, 300:364], scr[:, 364:428], OP.subtract))

            qbi = 0
            for p in range(HEADS // 2):
                h0, h1_ = 2 * p, 2 * p + 1
                g = p // 2
                for qb in range(NQB):
                    kc_max = min(2 * (qb + 1), NT)
                    si = 0
                    ua_, ub_ = u_sets[si]
                    qbi += 1

                    def emit_uzero():
                        V.after(te, uev_set[si])
                        TE.tick(te.matmul(ua_[:, 0:260], zrow[0:1, 0:128], zrow[0:1, 0:260],
                                          start=True, stop=False, skip_group_check=True))
                        TE.tick(te.matmul(ub_[:, 0:260], zrow[0:1, 0:128], zrow[0:1, 0:260],
                                          start=True, stop=False, skip_group_check=True))
                    pend = None

                    def emit_av(pk):
                        kc2, b2, ebm, msk = pk
                        if msk:
                            V.after(te, mask_done[b2])
                        else:
                            S.after(te, expd_at[b2])
                        for hi, h in enumerate([h0, h1_]):
                            bank = ua_ if hi == 0 else ub_
                            for s in range(2):
                                for qt in range(2):
                                    col = (s * 2 + qt) * 65
                                    last = (kc2 == kc_max - 1)
                                    mm = te.matmul(
                                        bank[:, col : col + 65],
                                        ebm[:, (hi * 2 + s) * 256 + qt * 128 : (hi * 2 + s) * 256 + qt * 128 + 128],
                                        arena[:, VA0 + kc2 * 260 + g * 65 : VA0 + kc2 * 260 + (g + 1) * 65],
                                        start=False, stop=last, skip_group_check=True)
                                    if hi == 1 and s == 1 and qt == 1:
                                        TE.tick(mm)
                        av_done[b2] = TE.n

                    expd_at = [0, 0, 0]
                    if qbi == 1:
                        emit_uzero()
                    for kc in range(kc_max):
                        b = kc % 3
                        sb_ = s_banks[b]
                        S.after(te, exp_done[b])
                        for ji, (h, s) in enumerate([(h0, 0), (h0, 1), (h1_, 0), (h1_, 1)]):
                            koff, kbase = kslot(h // 4, s)
                            qoff, qbase = qslot(h, s)
                            mm = te.matmul(sb_[:, ji * 256 : ji * 256 + 256],
                                           arena[:, koff + kc * 128 : koff + (kc + 1) * 128],
                                           arena[:, qoff + qb * 256 : qoff + (qb + 1) * 256],
                                           start=True, stop=True, skip_group_check=True)
                            if ji == 3:
                                TE.tick(mm)
                        sc_done = TE.n
                        TE.after(se, sc_done)
                        TE.after(se, av_done[b])
                        V.after(se, mask_done[b])
                        eb = expb[:, b * 1024 : (b + 1) * 1024]
                        S.tick(se.activation(eb, sb_[:], AF.Exp, scale=SCALE))
                        exp_done[b] = S.n
                        expd_at[b] = S.n
                        dd = kc - 2 * qb
                        masked = 0 <= dd <= 1
                        if masked:
                            S.after(ve, exp_done[b])
                            V.tick(ve.tensor_tensor(
                                eb.rearrange("p (a c) -> p a c", c=256),
                                eb.rearrange("p (a c) -> p a c", c=256),
                                maskb[:, dd * 256 : dd * 256 + 256].unsqueeze(1).broadcast_to([128, 4, 256]),
                                OP.mult))
                            mask_done[b] = V.n
                        if pend is not None:
                            emit_av(pend)
                        pend = (kc, b, eb, masked)
                        if kc == 0 and qbi > 1:
                            emit_combine()
                            emit_uzero()
                    if pend is not None:
                        emit_av(pend)
                    pend_cmb[0] = (h0, h1_, qb, ua_, ub_, si, TE.n)
            emit_combine()

        if upto == "attn":
            V.after(gp)
            DBG.tick(gp.dma_start(dbg_d["yatt"], y_all[:, 0 : 8 * T]))
            V.after(sy)
            OUT.tick(sy.dma_start(out_d[0:128, :], scr[:]))
            OUT.after(sy)
            DBG.after(sy)
            return nc

        if "yatt" in dbg:
            V.after(gp)
            DBG.tick(gp.dma_start(dbg_d["yatt"], y_all[:, 0 : 8 * T]))
            DBG.after(ve)

        with nc.psum_tensor("po0", [128, 512], F32) as po0, \
             nc.psum_tensor("po1", [128, 512], F32) as po1, \
             nc.psum_tensor("pty", [128, 1024], BF16) as pty, \
             nc.psum_tensor("pty2", [128, 1024], BF16) as pty2:
            xload2 = {}
            _o5 = NHC * CAP + ((NHC * CAP) & 1)
            xt = arena[:, _o5 : _o5 + 4096].bitcast(F32)
            outf = arena[:, _o5 + 4096 : _o5 + 6144].bitcast(F32)
            scr2 = arena[:, _o5 + 6144 : _o5 + 8192].bitcast(F32)
            ynTs2 = [arena[:, _o5 + 8192 : _o5 + 9216], arena[:, _o5 + 9216 : _o5 + 10240]]
            gread = arena[:, _o5 + 10240 : _o5 + 14336].bitcast(F32)
            GR = [Ctr(_sem("smGR0"), 16), Ctr(_sem("smGR1"), 16)]
            gload = {}
            SOT.after(te)
            V.after(sy)
            TE.after(sy)
            S.after(sy)
            SCAT[0].after(sy)
            SCAT[1].after(sy)
            for tt in range(min(2, NT)):
                GR[tt % 2].tick(sy.dma_start(gread[:, (tt % 2) * 1024 : (tt % 2 + 1) * 1024],
                                             g_d[tt * 128 : (tt + 1) * 128, :]))
                gload[tt] = GR[tt % 2].n
            for tt in range(min(2, NT)):
                X[tt % 2].tick(sy.dma_start(xt[:, (tt % 2) * 1024 : (tt % 2 + 1) * 1024],
                                            x_d[tt * 128 : (tt + 1) * 128, :]))
                xload2[tt] = X[tt % 2].n
            ptyev = 0
            pofree = [0, 0]
            scr2_read = [0, 0]
            st_prev = 0
            rmsst = arena[:, _o5 + 14336 : _o5 + 15360].bitcast(F32)
            for tt in range(NT):
                ysl = y_all[:, tt * 1024 : (tt + 1) * 1024]
                V.tick(ve.tensor_tensor(scr[:], ysl, ysl, OP.mult))
                V.tick(ve.reduce_sum(rmsst[:, tt * 16 : (tt + 1) * 16],
                                     scr[:].rearrange("p (h d) -> p h d", d=64), axis=AX.X))
            V.after(se)
            S.tick(se.activation(rmsst[:, 256 : 256 + 16 * NT], rmsst[:, 0 : 16 * NT],
                                 AF.Sqrt, bias=epsc[:, 1:2], scale=1.0 / HD))
            S.after(ve)
            V.tick(ve.reciprocal(rmsst[:, 0 : 16 * NT], rmsst[:, 256 : 256 + 16 * NT]))
            rms_v = V.n
            for tt in range(NT):
                ysl = y_all[:, tt * 1024 : (tt + 1) * 1024]
                V.tick(ve.tensor_tensor(ysl.rearrange("p (h d) -> p h d", d=64),
                                        ysl.rearrange("p (h d) -> p h d", d=64),
                                        rmsst[:, tt * 16 : (tt + 1) * 16].unsqueeze(2).broadcast_to([128, 16, 64]),
                                        OP.mult))
                V.tick(ve.tensor_tensor(ysl.rearrange("p (h d) -> p h d", d=64),
                                        ysl.rearrange("p (h d) -> p h d", d=64),
                                        rwb[:].unsqueeze(1).broadcast_to([128, 16, 64]), OP.mult))
            yn_done = V.n
            ptyev2 = [0, 0]
            ynT_done2 = [0, 0]
            ynT_read = [0, 0]

            def stage_t(tt):
                b = tt % 2
                ptyb = pty if b == 0 else pty2
                ysl = y_all[:, tt * 1024 : (tt + 1) * 1024]
                V.after(te, yn_done)
                V.after(te, ptyev2[b])
                for kc in range(8):
                    TE.tick(te.transpose(ptyb[:, kc * 128 : (kc + 1) * 128],
                                         ysl[:, kc * 128 : (kc + 1) * 128], identb[:]))
                TE.after(ve)
                if ynT_read[b]:
                    TE.after(ve, ynT_read[b])
                V.tick(ve.tensor_copy(ynTs2[b], ptyb[:]))
                ptyev2[b] = V.n
                ynT_done2[b] = V.n

            def stage_p(tt):
                nonlocal st_prev
                b = tt % 2
                for ch in range(2):
                    po = po0 if ch == 0 else po1
                    V.after(te, max(pofree[ch], ynT_done2[b]))
                    for kc in range(8):
                        TE.tick(te.matmul(po[:, 0:512], ynTs2[b][:, kc * 128 : (kc + 1) * 128],
                                          arena[:, WOUTT + kc * 1024 + ch * 512 : WOUTT + kc * 1024 + ch * 512 + 512],
                                          start=(kc == 0), stop=(kc == 7)))
                    s_done = TE.n
                    if ch == 1:
                        ynT_read[b] = TE.n
                    V.after(se, scr2_read[ch])
                    GR[tt % 2].after(se, gload[tt])
                    S.tick(se.activation(scr2[:, ch * 512 : ch * 512 + 512],
                                         gread[:, (tt % 2) * 1024 + ch * 512 : (tt % 2) * 1024 + ch * 512 + 512], AF.Silu))
                    sil_done = S.n
                    TE.after(ve, s_done)
                    S.after(ve, sil_done)
                    X[tt % 2].after(ve, xload2[tt])
                    OUT.after(ve, st_prev)
                    V.tick(ve.tensor_tensor(scr[:, ch * 512 : ch * 512 + 512], po[:, 0:512],
                                            scr2[:, ch * 512 : ch * 512 + 512], OP.mult))
                    scr2_read[ch] = V.n
                    V.tick(ve.tensor_tensor(outf[:, ch * 512 : ch * 512 + 512],
                                            scr[:, ch * 512 : ch * 512 + 512],
                                            xt[:, (tt % 2) * 1024 + ch * 512 : (tt % 2) * 1024 + ch * 512 + 512],
                                            OP.add))
                    pofree[ch] = V.n
                out_ready = V.n
                V.after(sy, out_ready)
                OUT.tick(sy.dma_start(out_d[tt * 128 : (tt + 1) * 128, :], outf[:]))
                st_prev = OUT.n
                if tt + 2 < NT:
                    V.after(sy, out_ready)
                    X[tt % 2].tick(sy.dma_start(xt[:, (tt % 2) * 1024 : (tt % 2 + 1) * 1024],
                                                x_d[(tt + 2) * 128 : (tt + 3) * 128, :]))
                    xload2[tt + 2] = X[tt % 2].n
                    S.after(sy)
                    GR[tt % 2].tick(sy.dma_start(gread[:, (tt % 2) * 1024 : (tt % 2 + 1) * 1024],
                                                 g_d[(tt + 2) * 128 : (tt + 3) * 128, :]))
                    gload[tt + 2] = GR[tt % 2].n

            stage_t(0)
            for tt in range(NT):
                if tt + 1 < NT:
                    stage_t(tt + 1)
                stage_p(tt)
            OUT.after(sy)
            DBG.after(sy)

    return nc


_CONSTS = {}


def _consts(T):
    if T not in _CONSTS:
        inv = 1.0 / 10000.0 ** (np.arange(0, HALF, 2, dtype=np.float32) / HALF)
        t = np.arange(T, dtype=np.float32)
        fr = np.outer(t, inv)
        r = np.arange(128)[:, None]
        c = np.arange(256)[None, :]
        mask = np.zeros((128, 512), dtype=np.float32)
        mask[:, 0:256] = (c - r >= 0).astype(np.float32)
        mask[:, 256:512] = (c - r >= 128).astype(np.float32)
        CAP = cap_for(T)
        ecap = np.zeros((128, 8), dtype=np.float32)
        for e in range(8):
            ecap[:, e] = e * CAP - 1
        NT = T // 128
        pfx = np.zeros((128, 128), dtype=np.float32)
        for t2 in range(NT):
            for tt in range(NT):
                if t2 < tt:
                    for e in range(8):
                        pfx[t2 * 8 + e, tt * 8 + e] = 1.0
        _CONSTS[T] = dict(
            cosc=np.cos(fr).astype(np.float32),
            sinc=np.sin(fr).astype(np.float32),
            ident=np.eye(128, dtype=np.float32),
            maskc=mask,
            onesc=np.ones((1, 128), dtype=np.float32),
            ltri=np.triu(np.ones((128, 128), dtype=np.float32)),
            onesq=np.ones((128, 128), dtype=np.float32),
            tokid=np.arange(T, dtype=np.int32).reshape(T // 128, 128).T.copy(),
            ecap=ecap,
            pfx=pfx,
        )
    return _CONSTS[T]


def make_inputs(x, Wq, Wkv, Wout, lam, rmsw, gw, w1, w3, w2, T):
    consts = _consts(T)
    shared = dict(
        wqt=np.ascontiguousarray(np.asarray(Wq, dtype=np.float32).T),
        wkvt=np.ascontiguousarray(np.asarray(Wkv, dtype=np.float32).T),
        woutt=np.ascontiguousarray(np.asarray(Wout, dtype=np.float32).T),
        lam=np.ascontiguousarray(lam, dtype=np.float32),
        rmsw=np.ascontiguousarray(rmsw, dtype=np.float32),
        gw=np.ascontiguousarray(gw, dtype=np.float32),
        w1t=np.ascontiguousarray(np.asarray(w1, dtype=np.float32).transpose(0, 2, 1)),
        w3t=np.ascontiguousarray(np.asarray(w3, dtype=np.float32).transpose(0, 2, 1)),
        w2t=np.ascontiguousarray(np.asarray(w2, dtype=np.float32).transpose(0, 2, 1)),
        **consts,
    )
    x = np.ascontiguousarray(x, dtype=np.float32)
    return [dict(x=x[b], **shared) for b in range(x.shape[0])]


def kernel(L0_attn_x, L0_attn_Wq, L0_attn_Wkv, L0_attn_Wout, L0_attn_lambda_init,
           L0_attn_rms_weight, L0_ffn_gate_w, L0_ffn_w1, L0_ffn_w3, L0_ffn_w2):
    T = 2048
    nc = build_program(T=T)
    in_maps = make_inputs(L0_attn_x, L0_attn_Wq, L0_attn_Wkv, L0_attn_Wout,
                          L0_attn_lambda_init, L0_attn_rms_weight, L0_ffn_gate_w,
                          L0_ffn_w1, L0_ffn_w3, L0_ffn_w2, T)
    res = run_bass_kernel_spmd(nc, in_maps, core_ids=list(range(8)))
    out = np.stack([res.results[b]["out"] for b in range(8)], axis=0)
    return out.astype(np.float32)
